# version 04
# baseline (speedup 1.0000x reference)
import numpy as np
import concourse.bass as bass
import concourse.mybir as mybir
from concourse.bass_utils import run_bass_kernel_spmd
from concourse.bass_types import AxisInfo

F32 = mybir.dt.float32
BF16 = mybir.dt.bfloat16
U32 = mybir.dt.uint32
I32 = mybir.dt.int32
AF = mybir.ActivationFunctionType
ALU = mybir.AluOpType
AX = mybir.AxisListType

ENGINES = ("sync", "tensor", "vector", "scalar", "gpsimd")
DMA_DEPTH = 8
PARANOID = False

B = 8
T = 4096
D = 1024
NCORES = 8



class _Buf:
    __slots__ = ("h", "last_w", "reads", "name", "_guard")

    def __init__(self, h, name):
        self.h = h
        self.name = name
        self.last_w = None
        self.reads = []

    def __getitem__(self, key):
        return Ref(self, self.h[key])

    @property
    def ap(self):
        return Ref(self, self.h[:])

    def rearrange(self, spec, **kw):
        return Ref(self, self.h.rearrange(spec, **kw))


class Ref:
    __slots__ = ("buf", "_ap")

    def __init__(self, buf, ap):
        self.buf = buf
        self._ap = ap

    def rearrange(self, spec, **kw):
        return Ref(self.buf, self._ap.rearrange(spec, **kw))

    def __getitem__(self, key):
        return Ref(self.buf, self._ap[key])


def _ref(x):
    if isinstance(x, Ref):
        return x
    if isinstance(x, _Buf):
        return x.ap
    return None


class Emit:
    def __init__(self, nc):
        self.nc = nc
        self.streams = {e: [] for e in ENGINES}
        self._guards = []
        self.sems = {}
        self.cnt = {}
        self.seen = {e: {} for e in ENGINES}
        self.chan = {}
        for e in ENGINES:
            for pre in ("p_", "d_"):
                n = pre + e
                self._mk_sem(n)
                self.chan[n] = n
        self.prev_dma = {}
        self.dma_rr = {}
        self._ccn = 0
        self._dbg = {}

    def _mk_sem(self, name):
        g = self.nc.semaphore(name)
        h = g.__enter__()
        self._guards.append(g)
        self.sems[name] = h
        self.cnt[name] = 0
        return h

    _name_n = 0

    def sbuf(self, name, shape, dtype):
        Emit._name_n += 1
        name = "%s_u%d" % (name, Emit._name_n)
        g = self.nc.sbuf_tensor(name, list(shape), dtype)
        h = g.__enter__() if hasattr(g, "__enter__") else g
        b = _Buf(h, name)
        b._guard = g
        return b

    _psum_n = 0

    def psum(self, shape, dtype=F32, name=None):
        if name is None:
            Emit._psum_n += 1
            name = "ps%d" % Emit._psum_n
        g = self.nc.psum_tensor(name, list(shape), dtype)
        h = g.__enter__() if hasattr(g, "__enter__") else g
        b = _Buf(h, "psum")
        b._guard = g
        return b

    def free(self, *bufs):
        for b in reversed(bufs):
            b._guard.__exit__(None, None, None)

    def dram(self, name, shape, dtype, kind="Internal", addr_space="Local"):
        h = self.nc.dram_tensor(name, list(shape), dtype, kind=kind, addr_space=addr_space)
        return _Buf(h, name)

    def _op(self, eng, fn, reads=(), writes=(), dma=False, desc=""):
        reads = [_ref(r) for r in reads if _ref(r) is not None]
        writes = [_ref(w) for w in writes if _ref(w) is not None]
        toks = []
        for r in reads:
            if r.buf.last_w:
                toks.append(r.buf.last_w)
        for w in writes:
            if w.buf.last_w:
                toks.append(w.buf.last_w)
            toks.extend(w.buf.reads)
        need = {}
        for (s, v) in toks:
            if need.get(s, 0) < v:
                need[s] = v
        my_p = "p_" + eng
        seen = self.seen[eng]
        for s, v in sorted(need.items()):
            if s == my_p and eng == "tensor":
                continue
            if seen.get(s, 0) >= v:
                continue
            sem = self.sems[s]
            self.streams[eng].append(lambda E, sem=sem, v=v: E.wait_ge(sem, v))
            seen[s] = v
        inc = 16 if dma else 1
        if dma:
            i = self.dma_rr.get(eng, 0)
            self.dma_rr[eng] = i + 1
            slot = "d_%s@%d" % (eng, i % DMA_DEPTH)
            sname = self.chan.get(slot)
            if sname is None:
                sname = slot
                self._mk_sem(sname)
                self.chan[slot] = sname
            if self.cnt[sname] + inc > 60000:
                sname = "%s#%d" % (slot, len([k for k in self.cnt if k.startswith(slot)]))
                self._mk_sem(sname)
                self.chan[slot] = sname
            prev = self.prev_dma.get(slot)
            if prev is not None and self.seen[eng].get(prev[0], 0) < prev[1]:
                sem_p = self.sems[prev[0]]
                self.streams[eng].append(lambda E, sem_p=sem_p, v_p=prev[1]: E.wait_ge(sem_p, v_p))
                self.seen[eng][prev[0]] = prev[1]
        else:
            chan = "p_" + eng
            sname = self.chan.get(chan, chan)
            if self.cnt[sname] + inc > 60000:
                sname = "%s#%d" % (chan, len([k for k in self.cnt if k.startswith(chan)]))
                self._mk_sem(sname)
                self.chan[chan] = sname
        self.cnt[sname] += inc
        val = self.cnt[sname]
        sem = self.sems[sname]
        self.streams[eng].append(lambda E, fn=fn, sem=sem, inc=inc: fn(E).then_inc(sem, inc))
        tok = (sname, val)
        if dma:
            self.prev_dma[slot] = tok
        for w in writes:
            w.buf.last_w = tok
            w.buf.reads = []
        for r in reads:
            r.buf.reads.append(tok)
        return tok

    def raw_wait(self, eng, tok):
        s, v = tok
        if self.seen[eng].get(s, 0) >= v:
            return
        sem = self.sems[s]
        self.streams[eng].append(lambda E, sem=sem, v=v: E.wait_ge(sem, v))
        self.seen[eng][s] = v

    def barrier(self):
        toks = [(n, self.cnt[n]) for n in self.cnt if self.cnt[n] > 0]
        for e in ENGINES:
            for t in toks:
                if t[0] == "p_" + e:
                    continue
                self.raw_wait(e, t)

    def load(self, dst, src, eng="sync", **kw):
        return self._op(eng, lambda E: E.dma_start(out=_ref(dst)._ap, in_=_ref(src)._ap, **kw),
                        reads=[src], writes=[dst], dma=True)

    def store(self, dst, src, eng="gpsimd", **kw):
        return self._op(eng, lambda E: E.dma_start(out=_ref(dst)._ap, in_=_ref(src)._ap, **kw),
                        reads=[src], writes=[dst], dma=True)

    def gather(self, dst, src_dram, idx, axis=0, bounds=None, eng="gpsimd"):
        def fn(E, dst=dst, src_dram=src_dram, idx=idx):
            try:
                return E.indirect_dma_start(
                    out=_ref(dst)._ap, out_offset=None,
                    in_=_ref(src_dram)._ap,
                    in_offset=bass.IndirectOffsetOnAxis(ap=_ref(idx)._ap, axis=axis),
                    bounds_check=bounds)
            except Exception:
                print("GATHER FAIL dst=", _ref(dst)._ap, "src=", _ref(src_dram)._ap,
                      "idx=", _ref(idx)._ap)
                raise
        return self._op(eng, fn, reads=[src_dram, idx], writes=[dst], dma=True)

    def scatter(self, dst_dram, src, idx, axis=0, bounds=None, eng="gpsimd"):
        def fn(E):
            return E.indirect_dma_start(
                out=_ref(dst_dram)._ap,
                out_offset=bass.IndirectOffsetOnAxis(ap=_ref(idx)._ap, axis=axis),
                in_=_ref(src)._ap, in_offset=None,
                bounds_check=bounds)
        return self._op(eng, fn, reads=[src, idx], writes=[dst_dram], dma=True)

    def mm(self, out, lhsT, rhs, start=True, stop=True, **kw):
        return self._op("tensor",
                        lambda E: E.matmul(_ref(out)._ap, _ref(lhsT)._ap, _ref(rhs)._ap,
                                           start=start, stop=stop, **kw),
                        reads=[lhsT, rhs], writes=[out])

    def transpose(self, out_psum, in_, ident):
        return self._op("tensor",
                        lambda E: E.transpose(_ref(out_psum)._ap, _ref(in_)._ap, _ref(ident)._ap),
                        reads=[in_, ident], writes=[out_psum])

    def act(self, out, in_, func, bias=0.0, scale=1.0, eng="scalar", accum_out=None):
        br = bias if isinstance(bias, (Ref, _Buf)) else None
        ao = accum_out
        def fn(E):
            kw = {}
            if ao is not None:
                kw["accum_out"] = _ref(ao)._ap
            return E.activation(_ref(out)._ap, _ref(in_)._ap, func,
                                bias=(_ref(bias)._ap if br is not None else bias),
                                scale=(_ref(scale)._ap if isinstance(scale, (Ref, _Buf)) else scale), **kw)
        reads = [in_] + ([bias] if br is not None else []) + ([scale] if isinstance(scale, (Ref, _Buf)) else [])
        writes = [out] + ([accum_out] if accum_out is not None else [])
        return self._op(eng, fn, reads=reads, writes=writes)

    def tt(self, out, a, b_, op, eng="vector"):
        return self._op(eng, lambda E: E.tensor_tensor(_ref(out)._ap, _ref(a)._ap, _ref(b_)._ap, op),
                        reads=[a, b_], writes=[out])

    def ts(self, out, in_, s1, op0, s2=None, op1=ALU.bypass, eng="vector"):
        sr = [x for x in (s1, s2) if isinstance(x, (Ref, _Buf))]
        def fn(E):
            a1 = _ref(s1)._ap if isinstance(s1, (Ref, _Buf)) else s1
            a2 = _ref(s2)._ap if isinstance(s2, (Ref, _Buf)) else s2
            if s2 is None:
                return E.tensor_single_scalar(_ref(out)._ap, _ref(in_)._ap, a1, op0)
            return E.tensor_scalar(_ref(out)._ap, _ref(in_)._ap, a1, a2, op0, op1)
        return self._op(eng, fn, reads=[in_] + sr, writes=[out])

    def stt(self, out, in0, scalar, in1, op0, op1, eng="vector"):
        sr = [scalar] if isinstance(scalar, (Ref, _Buf)) else []
        def fn(E):
            s = _ref(scalar)._ap if isinstance(scalar, (Ref, _Buf)) else scalar
            return E.scalar_tensor_tensor(_ref(out)._ap, _ref(in0)._ap, s, _ref(in1)._ap, op0, op1)
        return self._op(eng, fn, reads=[in0, in1] + sr, writes=[out])

    def scan(self, out, a, x, initial, op0=ALU.mult, op1=ALU.add, eng="vector"):
        init_r = [initial] if isinstance(initial, (Ref, _Buf)) else []
        def fn(E):
            iv = _ref(initial)._ap if isinstance(initial, (Ref, _Buf)) else initial
            return E.tensor_tensor_scan(_ref(out)._ap, _ref(a)._ap, _ref(x)._ap, iv, op0, op1)
        return self._op(eng, fn, reads=[a, x] + init_r, writes=[out])

    def red(self, out, in_, op, axis=AX.X, negate=False, eng="vector"):
        return self._op(eng,
                        lambda E: E.tensor_reduce(_ref(out)._ap, _ref(in_)._ap, axis, op, negate=negate),
                        reads=[in_], writes=[out])

    def copy(self, out, in_, eng="vector"):
        return self._op(eng, lambda E: E.tensor_copy(_ref(out)._ap, _ref(in_)._ap),
                        reads=[in_], writes=[out])

    def select(self, out, mask, t_, f_, eng="vector"):
        return self._op(eng, lambda E: E.select(_ref(out)._ap, _ref(mask)._ap, _ref(t_)._ap, _ref(f_)._ap),
                        reads=[mask, t_, f_], writes=[out])

    def recip(self, out, in_):
        return self._op("vector", lambda E: E.reciprocal(_ref(out)._ap, _ref(in_)._ap),
                        reads=[in_], writes=[out])

    def memset(self, dst, val, eng="gpsimd"):
        return self._op(eng, lambda E: E.memset(_ref(dst)._ap, val), writes=[dst])

    def iota(self, dst, pattern, base=0, channel_multiplier=0, eng="gpsimd"):
        return self._op(eng,
                        lambda E: E.iota(_ref(dst)._ap, pattern, base=base,
                                         channel_multiplier=channel_multiplier,
                                         allow_small_or_imprecise_dtypes=True),
                        writes=[dst])

    def affsel(self, out, in_, pattern, cmp, fill, base=0, channel_multiplier=0):
        return self._op("gpsimd",
                        lambda E: E.affine_select(_ref(out)._ap, _ref(in_)._ap, pattern, cmp, fill,
                                                  base=base, channel_multiplier=channel_multiplier),
                        reads=[in_], writes=[out])

    def collective(self, kind, op, ins, outs, replica_groups=None):
        if replica_groups is None:
            replica_groups = [list(range(NCORES))]
        name = "cc%d" % self._ccn
        self._ccn += 1
        sem = self._mk_sem(name)
        self.cnt[name] = 1
        def fn(E):
            return E.collective_compute(kind, op, replica_groups=replica_groups,
                                        ins=[_ref(i)._ap.opt() for i in ins],
                                        outs=[_ref(o)._ap.opt() for o in outs])
        reads, writes = list(ins), list(outs)
        toks = []
        for r in reads:
            rr = _ref(r)
            if rr.buf.last_w:
                toks.append(rr.buf.last_w)
        for w in writes:
            ww = _ref(w)
            if ww.buf.last_w:
                toks.append(ww.buf.last_w)
            toks.extend(ww.buf.reads)
        need = {}
        for (s, v) in toks:
            need[s] = max(need.get(s, 0), v)
        for s, v in sorted(need.items()):
            if self.seen["gpsimd"].get(s, 0) >= v:
                continue
            semw = self.sems[s]
            self.streams["gpsimd"].append(lambda E, semw=semw, v=v: E.wait_ge(semw, v))
            self.seen["gpsimd"][s] = v
        self.streams["gpsimd"].append(lambda E, fn=fn, sem=sem: fn(E).then_inc(sem))
        tok = (name, 1)
        for w in writes:
            ww = _ref(w)
            ww.buf.last_w = tok
            ww.buf.reads = []
        for r in reads:
            _ref(r).buf.reads.append(tok)
        return tok

    def dyn_region(self, eng, body_fn, n_dma):
        name = "sw%d" % self._ccn
        self._ccn += 1
        sem = self._mk_sem(name)
        self.cnt[name] = 16 * n_dma

        def cl(E, sem=sem):
            sv = self.nc.axis_index(AxisInfo(NCORES, 1), engines=[E.engine])
            k = [0]
            def mk(inst, sem=sem, k=k):
                inst.then_inc(sem, 16)
                k[0] += 1
                if k[0] < n_dma:
                    E.wait_ge(sem, 16 * k[0])
            body_fn(E, sv, mk)
        self.streams[eng].append(cl)
        return (name, 16 * n_dma)

    def dump(self, ref, name, shape=None, dtype=None):
        r = _ref(ref)
        ap = r._ap
        shape = shape or list(ap.shape)
        dtype = dtype or ap.dtype
        d = self.dram("dbg_" + name, shape, dtype, kind="ExternalOutput")
        self.store(d, r)
        self._dbg[name] = d
        return d

    def finalize(self):
        for s, v in self.cnt.items():
            if (s.startswith("d_") or s.startswith("cc")) and v > 0:
                self.raw_wait("gpsimd", (s, v))
        nc = self.nc
        with nc.Block() as block:
            for ename in ENGINES:
                stream = self.streams[ename]
                def mk(stream):
                    def body(E):
                        for f in stream:
                            f(E)
                    return body
                getattr(block, ename)(mk(stream))
        for g in reversed(self._guards):
            g.__exit__(None, None, None)



def make_consts(em):
    c = {}
    c["ones"] = em.sbuf("c_ones", [128, 128], F32)
    em.memset(c["ones"], 1.0)
    for nm, pat, cm, cmp in (
        ("ident", [[-1, 128]], 1, ALU.is_equal),
        ("trils", [[-1, 128]], 1, ALU.is_gt),
        ("triui", [[1, 128]], -1, ALU.is_ge),
        ("trius", [[1, 128]], -1, ALU.is_gt),
    ):
        c[nm] = em.sbuf("c_" + nm, [128, 128], F32)
        em.affsel(c[nm], c["ones"], pat, cmp, 0.0, base=0, channel_multiplier=cm)
    c["ones1"] = em.sbuf("c_ones1", [1, 512], F32)
    em.memset(c["ones1"], 1.0)
    c["ones_bf"] = em.sbuf("c_ones_bf", [128, 128], BF16)
    em.memset(c["ones_bf"], 1.0)
    return c



TCH = 512
NT = T // TCH
NC = D // 128


class Phase:

    def __init__(self, em):
        self.em = em
        self.bufs = []

    def __enter__(self):
        return self

    def sbuf(self, name, shape, dtype):
        b = self.em.sbuf(name, shape, dtype)
        self.bufs.append(b)
        return b

    def psum(self, shape, dtype=F32):
        b = self.em.psum(shape, dtype)
        self.bufs.append(b)
        return b

    def __exit__(self, *a):
        self.em.free(*self.bufs)
        self.em.barrier()


def load_col(em, ph, name, dram, ct, n=128):
    t = ph.sbuf(name, [n, 1], F32)
    em.load(t, dram[ct * 128:ct * 128 + n].rearrange("(a b) -> a b", b=1) if len(dram.h.shape) == 1 else dram[ct * 128:ct * 128 + n, :])
    return t


def load_weight_bf(em, ph, name, dram_ref, rows, cols):
    wbf = ph.sbuf(name, [128, (rows // 128) * cols], BF16)
    stage = ph.sbuf(name + "_st", [128, cols], F32)
    for r in range(rows // 128):
        em.load(stage, dram_ref[r * 128:(r + 1) * 128, :])
        em.act(wbf[:, r * cols:(r + 1) * cols], stage, AF.Copy)
    return _WView(wbf, cols)


class _WView:

    def __init__(self, buf, cols):
        self.buf = buf
        self.cols = cols

    def chunk(self, r, c0=0, c1=None):
        c1 = self.cols if c1 is None else c1
        return self.buf[:, r * self.cols + c0: r * self.cols + c1]


def ln_fm(em, c, src, dst_f32, dst_bf=None, eps=1e-5, gain=None, bias=None, sq_scale=1.0, dst_lo=None):
    with Phase(em) as ph:
        xt = [ph.sbuf("lnx%d" % i, [128, TCH], F32) for i in range(NC)]
        sq = ph.sbuf("lnsq", [128, TCH], F32)
        row = ph.sbuf("lnrow", [1, TCH], F32)
        row2 = ph.sbuf("lnrow2", [1, TCH], F32)
        row3 = ph.sbuf("lnrow3", [1, TCH], F32)
        ps_s0 = ph.psum([1, TCH])
        ps_s1 = ph.psum([1, TCH])
        ps_b = ph.psum([128, TCH])
        ps_b2 = ph.psum([128, TCH])
        xm = ph.sbuf("lnxm", [128, TCH], F32)
        xbf = ph.sbuf("lnxbf", [128, TCH], BF16)
        xlo = ph.sbuf("lnxlo", [128, TCH], BF16)
        gcols = bcols = None
        if gain is not None:
            gcols = [load_col(em, ph, "lng%d" % i, gain, i) for i in range(NC)]
        if bias is not None:
            bcols = [load_col(em, ph, "lnb%d" % i, bias, i) for i in range(NC)]
        for tc in range(NT):
            for i in range(NC):
                em.load(xt[i], src[i * 128:(i + 1) * 128, tc * TCH:(tc + 1) * TCH])
            for i in range(NC):
                em.mm(ps_s0, c["ones"][0:128, 0:1], xt[i], start=(i == 0), stop=(i == NC - 1))
            for i in range(NC):
                em.act(sq, xt[i], AF.Square)
                em.mm(ps_s1, c["ones"][0:128, 0:1], sq, start=(i == 0), stop=(i == NC - 1))
            em.ts(row, ps_s0[0:1, :], 1.0 / D, ALU.mult)
            em.ts(sq[0:1, :], ps_s1[0:1, :], 1.0 / D, ALU.mult)
            em.act(xm[0:1, :], row, AF.Square)
            em.tt(sq[0:1, :], sq[0:1, :], xm[0:1, :], ALU.subtract)
            em.ts(sq[0:1, :], sq[0:1, :], float(sq_scale), ALU.mult, float(eps), ALU.add)
            em.act(row3, sq[0:1, :], AF.Sqrt)
            em.recip(row2, row3)
            em.mm(ps_b, c["ones"][0:1, 0:128], row)
            em.mm(ps_b2, c["ones"][0:1, 0:128], row2)
            for i in range(NC):
                em.tt(xm, xt[i], ps_b, ALU.subtract)
                em.tt(xm, xm, ps_b2, ALU.mult)
                if gain is not None:
                    em.ts(xm, xm, gcols[i][:, 0:1], ALU.mult,
                          bcols[i][:, 0:1] if bias is not None else None,
                          ALU.add if bias is not None else ALU.bypass)
                if dst_f32 is not None:
                    em.store(dst_f32[i * 128:(i + 1) * 128, tc * TCH:(tc + 1) * TCH], xm)
                if dst_bf is not None:
                    em.act(xbf, xm, AF.Copy)
                    em.store(dst_bf[i * 128:(i + 1) * 128, tc * TCH:(tc + 1) * TCH], xbf)
                    if dst_lo is not None:
                        em.tt(sq, xm, xbf, ALU.subtract)
                        em.act(xlo, sq, AF.Copy)
                        em.store(dst_lo[i * 128:(i + 1) * 128, tc * TCH:(tc + 1) * TCH], xlo)


def cast_fm(em, src, dst_bf):
    with Phase(em) as ph:
        a = ph.sbuf("cfa", [128, T], F32)
        b_ = ph.sbuf("cfb", [128, T], BF16)
        for i in range(NC):
            em.load(a, src[i * 128:(i + 1) * 128, :])
            em.act(b_, a, AF.Copy)
            em.store(dst_bf[i * 128:(i + 1) * 128, :], b_)


def add3_fm(em, dst, a, b_, cc=None):
    with Phase(em) as ph:
        ta = ph.sbuf("a3a", [128, T], F32)
        tb = ph.sbuf("a3b", [128, T], F32)
        for i in range(NC):
            em.load(ta, a[i * 128:(i + 1) * 128, :])
            em.load(tb, b_[i * 128:(i + 1) * 128, :])
            em.tt(ta, ta, tb, ALU.add)
            if cc is not None:
                em.load(tb, cc[i * 128:(i + 1) * 128, :])
                em.tt(ta, ta, tb, ALU.add)
            em.store(dst[i * 128:(i + 1) * 128, :], ta)



def build_l1_ffn(em, c, io):
    xn = io["xn_fm"]
    out = io["out_fm"]
    W_r, W_k, W_v = io["l1f_WrT"], io["l1f_WkT"], io["l1f_WvT"]
    mu_r, mu_k = io["l1f_mu_r"], io["l1f_mu_k"]
    with Phase(em) as ph:
        wr = load_weight_bf(em, ph, "wr", W_r, D, D)
        wk = load_weight_bf(em, ph, "wk", W_k, D, D)
        wv = load_weight_bf(em, ph, "wv", W_v, D, D)
        xr = ph.sbuf("xr", [128, NC * TCH], BF16)
        xk = ph.sbuf("xk", [128, NC * TCH], BF16)
        r_ = ph.sbuf("r", [128, NC * TCH], BF16)
        ka = ph.sbuf("ka", [128, NC * TCH], BF16)
        xt = ph.sbuf("xt", [128, TCH + 1], F32)
        tmp = ph.sbuf("tmp", [128, TCH], F32)
        mu = [[load_col(em, ph, "mur%d" % i, mu_r, i) for i in range(NC)],
              [load_col(em, ph, "muk%d" % i, mu_k, i) for i in range(NC)]]
        s2 = [[ph.sbuf("s2r%d" % i, [128, 1], F32) for i in range(NC)],
              [ph.sbuf("s2k%d" % i, [128, 1], F32) for i in range(NC)]]
        for j in range(2):
            for i in range(NC):
                em.ts(s2[j][i], mu[j][i], -1.0, ALU.mult, 1.0, ALU.add)
        psum = [ph.psum([128, TCH]) for _ in range(2)]
        for tc in range(NT):
            lo = max(tc * TCH - 1, 0)
            w = tc * TCH + TCH - lo
            for i in range(NC):
                em.load(xt[:, 0:w], xn[i * 128:(i + 1) * 128, lo:tc * TCH + TCH])
                off = tc * TCH - lo
                for j, dst in ((0, xr), (1, xk)):
                    em.ts(tmp, xt[:, off:off + TCH], mu[j][i][:, 0:1], ALU.mult)
                    if off == 1:
                        em.stt(dst[:, i * TCH:i * TCH + TCH], xt[:, 0:TCH], s2[j][i][:, 0:1], tmp, ALU.mult, ALU.add)
                    else:
                        em.stt(dst[:, i * TCH + 1:i * TCH + TCH], xt[:, 0:TCH - 1], s2[j][i][:, 0:1], tmp[:, 1:TCH], ALU.mult, ALU.add)
                        em.copy(dst[:, i * TCH:i * TCH + 1], tmp[:, 0:1])
            for o in range(NC):
                for i in range(NC):
                    em.mm(psum[0], wr.chunk(i, o * 128, (o + 1) * 128), xr[:, i * TCH:(i + 1) * TCH],
                          start=(i == 0), stop=(i == NC - 1))
                em.act(r_[:, o * TCH:(o + 1) * TCH], psum[0], AF.Sigmoid)
                for i in range(NC):
                    em.mm(psum[1], wk.chunk(i, o * 128, (o + 1) * 128), xk[:, i * TCH:(i + 1) * TCH],
                          start=(i == 0), stop=(i == NC - 1))
                em.ts(tmp, psum[1], 0.0, ALU.max, eng="vector")
                em.act(ka[:, o * TCH:(o + 1) * TCH], tmp, AF.Square)
            for o in range(NC):
                for i in range(NC):
                    em.mm(psum[0], wv.chunk(i, o * 128, (o + 1) * 128), ka[:, i * TCH:(i + 1) * TCH],
                          start=(i == 0), stop=(i == NC - 1))
                em.tt(tmp, psum[0], r_[:, o * TCH:(o + 1) * TCH], ALU.mult)
                em.store(out[o * 128:(o + 1) * 128, tc * TCH:(tc + 1) * TCH], tmp)



def build_l0_attn(em, c, io):
    xn = io["xn_fm"]; xnb = io["xn_bf"]; out = io["out_fm"]
    w_in, b_in = io["l0a_w_in"], io["l0a_b_in"]
    w_a, b_a = io["l0a_w_a"], io["l0a_b_a"]
    a_param = io["l0a_a_param"]
    with Phase(em) as ph:
        xb = ph.sbuf("xb", [128, T], BF16)
        xf = ph.sbuf("xf", [128, T], F32)
        gx = ph.sbuf("gx", [128, T], F32)
        ga = ph.sbuf("ga", [128, T], F32)
        nx = ph.sbuf("nx", [128, T], F32)
        yy = ph.sbuf("yy", [128, T], F32)
        wi = ph.sbuf("wi", [128, 128], BF16)
        wa = ph.sbuf("wa", [128, 128], BF16)
        wst = ph.sbuf("wst", [128, 128], F32)
        bi = ph.sbuf("bi", [128, 1], F32)
        ba = ph.sbuf("ba", [128, 1], F32)
        apc = ph.sbuf("apc", [128, 1], F32)
        sc = ph.sbuf("sc", [128, 1], F32)
        ps = ph.psum([128, TCH])
        for h in range(8):
            em.load(xb, xnb[h * 128:(h + 1) * 128, :])
            em.load(xf, xn[h * 128:(h + 1) * 128, :])
            em.load(wst, w_in[h])
            em.act(wi, wst, AF.Copy)
            em.load(wst, w_a[h])
            em.act(wa, wst, AF.Copy)
            em.load(bi, b_in[h].rearrange("(a b) -> a b", b=1))
            em.load(ba, b_a[h].rearrange("(a b) -> a b", b=1))
            apc_src = a_param[h * 128:(h + 1) * 128].rearrange("(a b) -> a b", b=1)
            em.load(apc, apc_src)
            em.act(sc, apc, AF.Exp)
            em.act(apc, sc, AF.Ln, bias=1.0)
            em.ts(sc, apc, -8.0, ALU.mult)
            for tc in range(NT):
                em.mm(ps, wi, xb[:, tc * TCH:(tc + 1) * TCH])
                em.act(gx[:, tc * TCH:(tc + 1) * TCH], ps, AF.Sigmoid, bias=bi[:, 0:1])
                em.mm(ps, wa, xb[:, tc * TCH:(tc + 1) * TCH])
                em.act(ga[:, tc * TCH:(tc + 1) * TCH], ps, AF.Sigmoid, bias=ba[:, 0:1])
            em.ts(ga, ga, sc[:, 0:1], ALU.mult)
            em.act(yy, ga, AF.Exp, scale=2.0)
            em.ts(yy, yy, -1.0, ALU.mult, 1.0, ALU.add)
            em.act(nx, yy, AF.Sqrt)
            em.memset(nx[:, 0:1], 1.0, eng="vector")
            em.tt(gx, xf, gx, ALU.mult)
            em.tt(gx, gx, nx, ALU.mult)
            em.act(yy, ga, AF.Exp)
            em.memset(yy[:, 0:1], 0.0, eng="vector")
            em.scan(nx, yy, gx, 0.0)
            em.store(out[h * 128:(h + 1) * 128, :], nx)



def build_l3_attn(em, c, io):
    xn = io["xn_fm"]; xnb_d = io["xn_bf"]; xnl_d = io["xn_lo"]; out = io["out_fm"]
    Wg, bg = io["l3a_Wg"], io["l3a_bg"]
    Wv, bv = io["l3a_Wv"], io["l3a_bv"]
    Wd, bd = io["l3a_Wd"], io["l3a_bd"]
    gw = io["l3a_gate_W"]
    comb_dr = io["l3a_comb_scratch"]
    a_dr = io["l3a_a_scratch"]
    M = 4
    with Phase(em) as ph:
        xnb = ph.sbuf("xnb", [128, NC * T], BF16)
        for i in range(NC):
            em.load(xnb[:, i * T:(i + 1) * T], xnb_d[i * 128:(i + 1) * 128, :])
        gwt = ph.sbuf("gwt", [128, NC * 4], BF16)
        gwl = ph.sbuf("gwl", [128, NC * 4], BF16)
        gst = ph.sbuf("gst", [128, 4], F32)
        for i in range(NC):
            em.load(gst, gw[i * 128:(i + 1) * 128, :])
            em.act(gwt[:, i * 4:(i + 1) * 4], gst, AF.Copy)
            em.tt(gst, gst, gwt[:, i * 4:(i + 1) * 4], ALU.subtract)
            em.act(gwl[:, i * 4:(i + 1) * 4], gst, AF.Copy)
        xlo_t = ph.sbuf("xlot", [128, 128], BF16)
        psl = ph.psum([128, 4])
        pst = ph.psum([M, 128])
        comb_fm = ph.sbuf("combfm", [M, 128], F32)
        lg = ph.sbuf("lg", [128, 4], F32)
        v1 = ph.sbuf("v1", [128, 1], F32)
        v2 = ph.sbuf("v2", [128, 1], F32)
        m1 = ph.sbuf("m1", [128, 4], F32)
        m2 = ph.sbuf("m2", [128, 4], F32)
        cmb = ph.sbuf("cmb", [128, 4], F32)
        for tt_ in range(T // 128):
            for i in range(NC):
                em.load(xlo_t, xnl_d[i * 128:(i + 1) * 128, tt_ * 128:(tt_ + 1) * 128])
                em.mm(psl, xnb[:, i * T + tt_ * 128: i * T + tt_ * 128 + 128],
                      gwt[:, i * 4:(i + 1) * 4], start=(i == 0), stop=False)
                em.mm(psl, xnb[:, i * T + tt_ * 128: i * T + tt_ * 128 + 128],
                      gwl[:, i * 4:(i + 1) * 4], start=False, stop=False)
                em.mm(psl, xlo_t, gwt[:, i * 4:(i + 1) * 4],
                      start=False, stop=(i == NC - 1))
            em.copy(lg, psl)
            em.red(v1, lg, ALU.max)
            em.ts(m1, lg, v1[:, 0:1], ALU.is_ge)
            em.stt(m2, m1, -1e30, lg, ALU.mult, ALU.add)
            em.red(v2, m2, ALU.max)
            em.ts(m2, m2, v2[:, 0:1], ALU.is_ge)
            em.tt(cmb[:, 0:1], v1, v2, ALU.subtract)
            em.act(cmb[:, 1:2], cmb[:, 0:1], AF.Sigmoid)
            em.ts(cmb[:, 2:3], cmb[:, 1:2], -1.0, ALU.mult, 1.0, ALU.add)
            em.ts(m1, m1, cmb[:, 1:2], ALU.mult)
            em.ts(m2, m2, cmb[:, 2:3], ALU.mult)
            em.tt(cmb, m1, m2, ALU.add)
            em.transpose(pst, cmb, c["ident"])
            em.copy(comb_fm, pst)
            em.store(comb_dr[:, tt_ * 128:(tt_ + 1) * 128], comb_fm)
        crow = ph.sbuf("crow", [1, T], F32)
        if io.get("_dbg_gate_only"):
            return
        wA = ph.sbuf("wA", [128, NC * D], BF16)
        wB = ph.sbuf("wB", [128, NC * D], BF16)
        wL = ph.sbuf("wL", [128, NC * D], BF16)
        stg = ph.sbuf("stg", [128, D], F32)
        t1 = ph.sbuf("t1", [128, T], F32)
        t2 = ph.sbuf("t2", [128, T], F32)
        t3 = ph.sbuf("t3", [128, T], F32)
        bcol = ph.sbuf("bcol", [128, 1], F32)
        shi = ph.sbuf("shi", [128, TCH], BF16)
        slo = ph.sbuf("slo", [128, TCH], BF16)
        ps = ph.psum([128, TCH])
        psb = ph.psum([128, TCH])
        MLIM = io.get("_dbg_mlim", M)
        DTLIM = io.get("_dbg_dtlim", NC)
        for m in range(MLIM):
            em.load(crow, comb_dr[m:m + 1, :])
            for i in range(NC):
                em.load(stg, Wd[m, i * 128:(i + 1) * 128, :])
                em.act(wA[:, i * D:(i + 1) * D], stg, AF.Copy)
                em.tt(stg, stg, wA[:, i * D:(i + 1) * D], ALU.subtract)
                em.act(wL[:, i * D:(i + 1) * D], stg, AF.Copy)
            for tc in range(NT):
                for dt in range(DTLIM):
                    em.load(bcol, bd[m, dt * 128:dt * 128 + 128].rearrange("(a b) -> a b", b=1))
                    for i in range(NC):
                        em.load(shi, xnb_d[i * 128:(i + 1) * 128, tc * TCH:(tc + 1) * TCH])
                        em.load(slo, xnl_d[i * 128:(i + 1) * 128, tc * TCH:(tc + 1) * TCH])
                        em.mm(ps, wA[:, i * D + dt * 128: i * D + dt * 128 + 128], shi,
                              start=(i == 0), stop=False)
                        em.mm(ps, wL[:, i * D + dt * 128: i * D + dt * 128 + 128], shi,
                              start=False, stop=False)
                        em.mm(ps, wA[:, i * D + dt * 128: i * D + dt * 128 + 128], slo,
                              start=False, stop=(i == NC - 1))
                    em.act(shi_f := t1[:, 0:TCH], ps, AF.Sigmoid, bias=bcol[:, 0:1])
                    em.ts(t1[:, 0:TCH], t1[:, 0:TCH], 0.998, ALU.mult, 0.001, ALU.add)
                    em.store(a_dr[dt * 128:(dt + 1) * 128, tc * TCH:(tc + 1) * TCH], t1[:, 0:TCH])
            for wt, wd in ((wA, Wg), (wB, Wv)):
                for i in range(NC):
                    em.load(stg, wd[m, i * 128:(i + 1) * 128, :])
                    em.act(wt[:, i * D:(i + 1) * D], stg, AF.Copy)
            for dt in range(DTLIM):
                for proj, wt, bb, fn, dst in (
                        (0, wA, bg, AF.Sigmoid, t1), (1, wB, bv, AF.Tanh, t2)):
                    em.load(bcol, bb[m, dt * 128:dt * 128 + 128].rearrange("(a b) -> a b", b=1))
                    for tc in range(NT):
                        for i in range(NC):
                            em.mm(ps, wt[:, i * D + dt * 128: i * D + dt * 128 + 128],
                                  xnb[:, i * T + tc * TCH: i * T + tc * TCH + TCH],
                                  start=(i == 0), stop=(i == NC - 1))
                        em.act(dst[:, tc * TCH:(tc + 1) * TCH], ps, fn, bias=bcol[:, 0:1])
                em.tt(t2, t1, t2, ALU.mult)
                em.load(t1, a_dr[dt * 128:(dt + 1) * 128, :])
                em.scan(t3, t1, t2, 0.0)
                if m > 0:
                    em.load(t1, out[dt * 128:(dt + 1) * 128, :])
                for tc in range(NT):
                    em.mm(psb, c["ones"][0:1, 0:128], crow[0:1, tc * TCH:(tc + 1) * TCH])
                    em.tt(t2[:, tc * TCH:(tc + 1) * TCH], t3[:, tc * TCH:(tc + 1) * TCH], psb, ALU.mult)
                if m == 0:
                    em.store(out[dt * 128:(dt + 1) * 128, :], t2)
                else:
                    em.tt(t2, t2, t1, ALU.add)
                    em.store(out[dt * 128:(dt + 1) * 128, :], t2)


def rms_fm(em, c, src, dst_f32, dst_bf=None, dst_lo=None, eps=1e-6, gain=None):
    with Phase(em) as ph:
        xt = [ph.sbuf("rmx%d" % i, [128, TCH], F32) for i in range(NC)]
        sq = ph.sbuf("rmsq", [128, TCH], F32)
        row = ph.sbuf("rmrow", [1, TCH], F32)
        row2 = ph.sbuf("rmrow2", [1, TCH], F32)
        ps_s = ph.psum([1, TCH])
        ps_b = ph.psum([128, TCH])
        xm = ph.sbuf("rmxm", [128, TCH], F32)
        xbf = ph.sbuf("rmxbf", [128, TCH], BF16)
        xlo = ph.sbuf("rmxlo", [128, TCH], BF16)
        gcols = None
        if gain is not None:
            gcols = [load_col(em, ph, "rmg%d" % i, gain, i) for i in range(NC)]
        for tc in range(NT):
            for i in range(NC):
                em.load(xt[i], src[i * 128:(i + 1) * 128, tc * TCH:(tc + 1) * TCH])
            for i in range(NC):
                em.act(sq, xt[i], AF.Square)
                em.mm(ps_s, c["ones"][0:128, 0:1], sq, start=(i == 0), stop=(i == NC - 1))
            em.ts(row, ps_s[0:1, :], 1.0 / D, ALU.mult, float(eps), ALU.add)
            em.act(row2, row, AF.Sqrt)
            em.recip(row, row2)
            em.mm(ps_b, c["ones"][0:1, 0:128], row)
            for i in range(NC):
                em.tt(xm, xt[i], ps_b, ALU.mult)
                if gain is not None:
                    em.ts(xm, xm, gcols[i][:, 0:1], ALU.mult)
                if dst_f32 is not None:
                    em.store(dst_f32[i * 128:(i + 1) * 128, tc * TCH:(tc + 1) * TCH], xm)
                if dst_bf is not None:
                    em.act(xbf, xm, AF.Copy)
                    em.store(dst_bf[i * 128:(i + 1) * 128, tc * TCH:(tc + 1) * TCH], xbf)
                    if dst_lo is not None:
                        em.tt(sq, xm, xbf, ALU.subtract)
                        em.act(xlo, sq, AF.Copy)
                        em.store(dst_lo[i * 128:(i + 1) * 128, tc * TCH:(tc + 1) * TCH], xlo)


def topk_rows(em, c, ph, xbf_res, xlo_dram, wT_dram, bias_dram, E, rows_dram, renorm,
              psl, pst, prefix, n_free_t=True, rows_bf_dram=None):
    gwt = ph.sbuf(prefix + "gwt", [128, NC * E], BF16)
    gwl = ph.sbuf(prefix + "gwl", [128, NC * E], BF16)
    gst = ph.sbuf(prefix + "gst", [128, E], F32)
    for i in range(NC):
        em.load(gst, wT_dram[i * 128:(i + 1) * 128, :])
        em.act(gwt[:, i * E:(i + 1) * E], gst, AF.Copy)
        em.tt(gst, gst, gwt[:, i * E:(i + 1) * E], ALU.subtract)
        em.act(gwl[:, i * E:(i + 1) * E], gst, AF.Copy)
    brow = ph.sbuf(prefix + "brow", [1, E], F32)
    if bias_dram is not None:
        em.load(brow, bias_dram.rearrange("(a b) -> a b", a=1))
    xlo_t = ph.sbuf(prefix + "xlo", [128, 128], BF16)
    lg = ph.sbuf(prefix + "lg", [128, E], F32)
    pr = ph.sbuf(prefix + "pr", [128, E], F32)
    v1 = ph.sbuf(prefix + "v1", [128, 1], F32)
    v2 = ph.sbuf(prefix + "v2", [128, 1], F32)
    m1 = ph.sbuf(prefix + "m1", [128, E], F32)
    m2 = ph.sbuf(prefix + "m2", [128, E], F32)
    cmb = ph.sbuf(prefix + "cmb", [128, E], F32)
    ex = ph.sbuf(prefix + "ex", [128, E], F32)
    exT = ph.sbuf(prefix + "exT", [E, 128], F32)
    exTb = ph.sbuf(prefix + "exTb", [E, 128], BF16)
    for tt_ in range(T // 128):
        for i in range(NC):
            em.load(xlo_t, xlo_dram[i * 128:(i + 1) * 128, tt_ * 128:(tt_ + 1) * 128])
            em.mm(psl, xbf_res[:, i * T + tt_ * 128: i * T + tt_ * 128 + 128],
                  gwt[:, i * E:(i + 1) * E], start=(i == 0), stop=False)
            em.mm(psl, xbf_res[:, i * T + tt_ * 128: i * T + tt_ * 128 + 128],
                  gwl[:, i * E:(i + 1) * E], start=False, stop=False)
            last = (i == NC - 1) and bias_dram is None
            em.mm(psl, xlo_t, gwt[:, i * E:(i + 1) * E], start=False, stop=last)
        if bias_dram is not None:
            em.mm(psl, c["ones"][0:1, 0:128], brow, start=False, stop=True)
        em.copy(lg, psl)
        em.red(v1, lg, ALU.max)
        em.ts(v2, v1, -1.0, ALU.mult)
        em.act(ex, lg, AF.Exp, bias=v2[:, 0:1])
        em.red(v2, ex, ALU.add)
        em.recip(v2, v2)
        em.ts(pr, ex, v2[:, 0:1], ALU.mult)
        em.red(v1, lg, ALU.max)
        em.ts(m1, lg, v1[:, 0:1], ALU.is_ge)
        em.stt(m2, m1, -1e30, lg, ALU.mult, ALU.add)
        em.red(v2, m2, ALU.max)
        em.ts(m2, m2, v2[:, 0:1], ALU.is_ge)
        if renorm:
            em.tt(cmb, m1, m2, ALU.add)
            em.tt(cmb, cmb, pr, ALU.mult)
            em.red(v1, cmb, ALU.add)
            em.recip(v1, v1)
            em.ts(cmb, cmb, v1[:, 0:1], ALU.mult)
        else:
            em.tt(cmb, m1, m2, ALU.add)
            em.tt(cmb, cmb, pr, ALU.mult)
        em.transpose(pst, cmb, c["ident"])
        em.copy(exT, pst)
        em.store(rows_dram[:, tt_ * 128:(tt_ + 1) * 128], exT)
        if rows_bf_dram is not None:
            em.act(exTb, exT, AF.Copy)
            em.store(rows_bf_dram[:, tt_ * 128:(tt_ + 1) * 128], exTb)



_cast_tiles = {}

def cast_dram_bf(em, ph, src, dst, rows, cols, prefix):
    key = (id(ph), cols)
    if key not in _cast_tiles:
        _cast_tiles[key] = (ph.sbuf(prefix + "stg", [128, cols], F32),
                            ph.sbuf(prefix + "stb", [128, cols], BF16))
    stg, stb = _cast_tiles[key]
    for r in range(rows // 128):
        em.load(stg, src[r * 128:(r + 1) * 128, :])
        em.act(stb, stg, AF.Copy)
        em.store(dst[r * 128:(r + 1) * 128, :], stb)


class TilePool:
    def __init__(self, ph, prefix, n, shape, dtype):
        self.tiles = [ph.sbuf("%sp%d" % (prefix, i), shape, dtype) for i in range(n)]
        self.i = 0

    def next(self):
        t = self.tiles[self.i % len(self.tiles)]
        self.i += 1
        return t


def build_l2_ffn(em, c, io):
    xnb_d = io["xn_bf"]; xnl_d = io["xn_lo"]; out = io["out_fm"]
    Wsi = io["l2f_w_shared_in"]
    Wso = io["l2f_w_shared_out"]
    W1s = io["l2f_w1_shared"]
    W2 = io["l2f_w2"]
    Wgt = io["l2f_w_gate"]
    gbias = io["l2f_gate_bias"]
    wrows = io["l2f_wd_scratch"]
    wbf = io["l2f_wbf_scratch"]
    E = 8
    with Phase(em) as ph:
        xnb = ph.sbuf("l2xnb", [128, NC * T], BF16)
        for i in range(NC):
            em.load(xnb[:, i * T:(i + 1) * T], xnb_d[i * 128:(i + 1) * 128, :])
        psl = ph.psum([128, E])
        pst = ph.psum([E, 128])
        wrows_bf = io["l2f_wdbf_scratch"]
        topk_rows(em, c, ph, xnb, xnl_d, Wgt, gbias, E, wrows, False, psl, pst, "l2g",
                  rows_bf_dram=wrows_bf)
        cast_dram_bf(em, ph, W2.rearrange("e a b -> (e a) b"), wbf[0:8192, :], 8192, 1024, "cw2")
        cast_dram_bf(em, ph, Wsi[:, 0:1024], wbf[8192:9216, :], 1024, 1024, "cwsa")
        cast_dram_bf(em, ph, Wsi[:, 1024:2048], wbf[9216:10240, :], 1024, 1024, "cwsb")
        cast_dram_bf(em, ph, W1s[:, 0:1024], wbf[10240:11264, :], 1024, 1024, "cw1a")
        cast_dram_bf(em, ph, W1s[:, 1024:2048], wbf[11264:12288, :], 1024, 1024, "cw1b")
        cast_dram_bf(em, ph, Wso, wbf[12288:13312, :], 1024, 1024, "cwso")
        W2O = 0; WSA = 8192; WSB = 9216; W1A = 10240; W1B = 11264; WSO = 12288

        pool = TilePool(ph, "l2w", 18, [128, 512], BF16)
        hid = ph.sbuf("l2hid", [128, NC * TCH], BF16)
        swt = ph.sbuf("l2swt", [128, NC * TCH], BF16)
        hsc = ph.sbuf("l2hsc", [128, E * NC * TCH], BF16)
        ta = ph.sbuf("l2ta", [128, TCH], F32)
        tb = ph.sbuf("l2tb", [128, TCH], F32)
        crow = ph.sbuf("l2crow", [1, T], BF16)
        ps = ph.psum([128, TCH])
        ps2 = ph.psum([128, TCH])
        psb = ph.psum([128, TCH])
        psc = ph.psum([128, TCH])
        psd = ph.psum([128, TCH])
        ps4 = [ps, psb, psc, psd]
        hscv = hsc
        for tc in range(NT):
            for WA, WB, dst in ((WSA, WSB, swt), (W1A, W1B, hid)):
                for rnd in range(2):
                    tA = [pool.next() for _ in range(NC)]
                    tB = [pool.next() for _ in range(NC)]
                    for i in range(NC):
                        em.load(tA[i], wbf[WA + i * 128:WA + (i + 1) * 128,
                                           rnd * 512:(rnd + 1) * 512])
                        em.load(tB[i], wbf[WB + i * 128:WB + (i + 1) * 128,
                                           rnd * 512:(rnd + 1) * 512])
                    for o4 in range(4):
                        o = rnd * 4 + o4
                        for i in range(NC):
                            em.mm(ps, tA[i][:, o4 * 128:(o4 + 1) * 128],
                                  xnb[:, i * T + tc * TCH: i * T + tc * TCH + TCH],
                                  start=(i == 0), stop=(i == NC - 1))
                        for i in range(NC):
                            em.mm(ps2, tB[i][:, o4 * 128:(o4 + 1) * 128],
                                  xnb[:, i * T + tc * TCH: i * T + tc * TCH + TCH],
                                  start=(i == 0), stop=(i == NC - 1))
                        em.act(ta, ps, AF.Silu)
                        em.tt(tb, ta, ps2, ALU.mult)
                        em.act(dst[:, o * TCH:(o + 1) * TCH], tb, AF.Copy)
            for e in range(E):
                em.load(crow, wrows_bf[e:e + 1, :])
                em.mm(psb, c["ones_bf"][0:1, 0:128], crow[0:1, tc * TCH:(tc + 1) * TCH])
                for i in range(NC):
                    em.tt(hscv[:, (e * NC + i) * TCH:(e * NC + i + 1) * TCH],
                          hid[:, i * TCH:(i + 1) * TCH], psb, ALU.mult)
            for rnd in range(2):
                for e in range(E):
                    for i in range(NC):
                        w1 = pool.next()
                        em.load(w1[:, 0:512], wbf[(e * 8 + i) * 128:(e * 8 + i + 1) * 128,
                                                  rnd * 512:(rnd + 1) * 512])
                        for o4 in range(4):
                            em.mm(ps4[o4], w1[:, o4 * 128:o4 * 128 + 128],
                                  hscv[:, (e * NC + i) * TCH:(e * NC + i + 1) * TCH],
                                  start=(e == 0 and i == 0), stop=(e == E - 1 and i == NC - 1))
                tso = [pool.next() for _ in range(NC)]
                for i in range(NC):
                    em.load(tso[i], wbf[WSO + i * 128:WSO + (i + 1) * 128,
                                        rnd * 512:(rnd + 1) * 512])
                for o4 in range(4):
                    o = rnd * 4 + o4
                    for i in range(NC):
                        em.mm(ps2, tso[i][:, o4 * 128:(o4 + 1) * 128],
                              swt[:, i * TCH:(i + 1) * TCH],
                              start=(i == 0), stop=(i == NC - 1))
                    em.copy(ta, ps4[o4])
                    em.tt(ta, ta, ps2, ALU.add)
                    em.store(out[o * 128:(o + 1) * 128, tc * TCH:(tc + 1) * TCH], ta)



def build_l3_ffn(em, c, io):
    xn = io["xn_fm"]; out = io["out_fm"]
    g1, g2 = io["l3f_g1"], io["l3f_g2"]
    WqT, WkcT, WvcT, WoT = io["l3f_WqT"], io["l3f_WkcT"], io["l3f_WvcT"], io["l3f_WoT"]
    WrT = io["l3f_WrT"]
    ebias = io["l3f_expert_bias"]
    sW1T, sW2T, sW3T = io["l3f_sW1T"], io["l3f_sW2T"], io["l3f_sW3T"]
    eW1T, eW2T, eW3T = io["l3f_eW1T"], io["l3f_eW2T"], io["l3f_eW3T"]
    nx_f, nx_b, nx_l = io["l3f_nx_f"], io["l3f_nx_b"], io["l3f_nx_l"]
    x2_f = io["l3f_x2_f"]
    qb_d = io["l3f_q_scratch"]
    kb_d = io["l3f_k_scratch"]
    vtm_d = io["l3f_vtm_scratch"]
    crows = io["l3f_crows"]
    crows_f32 = io["l3f_crows_f32"]
    sh_d = io["l3f_sh_scratch"]
    wbf = io["l3f_wbf_scratch"]
    E = 8
    H = 4
    rms_fm(em, c, xn, nx_f, nx_b, nx_l, eps=1e-6, gain=g1)
    with Phase(em) as ph:
        nxb = ph.sbuf("f3nxb", [128, NC * T], BF16)
        for i in range(NC):
            em.load(nxb[:, i * T:(i + 1) * T], nx_b[i * 128:(i + 1) * 128, :])
        wq = load_weight_bf(em, ph, "f3wq", WqT, D, D)
        wk = load_weight_bf(em, ph, "f3wk", WkcT, D, 256)
        wv = load_weight_bf(em, ph, "f3wv", WvcT, D, 256)
        ps = ph.psum([128, TCH])
        pst = ph.psum([128, 128], BF16)
        qt = ph.sbuf("f3qt", [128, TCH], BF16)
        vev = ph.sbuf("f3vev", [128, 128], BF16)
        idb = ph.sbuf("f3idb", [128, 128], BF16)
        em.act(idb, c["ident"], AF.Copy)
        for tc in range(NT):
            for o in range(NC):
                for i in range(NC):
                    em.mm(ps, wq.chunk(i, o * 128, (o + 1) * 128),
                          nxb[:, i * T + tc * TCH: i * T + tc * TCH + TCH],
                          start=(i == 0), stop=(i == NC - 1))
                em.act(qt, ps, AF.Copy)
                em.store(qb_d[o * 128:(o + 1) * 128, tc * TCH:(tc + 1) * TCH], qt)
            for o in range(2):
                for i in range(NC):
                    em.mm(ps, wk.chunk(i, o * 128, (o + 1) * 128),
                          nxb[:, i * T + tc * TCH: i * T + tc * TCH + TCH],
                          start=(i == 0), stop=(i == NC - 1))
                em.act(qt, ps, AF.Copy)
                em.store(kb_d[o * 128:(o + 1) * 128, tc * TCH:(tc + 1) * TCH], qt)
                for i in range(NC):
                    em.mm(ps, wv.chunk(i, o * 128, (o + 1) * 128),
                          nxb[:, i * T + tc * TCH: i * T + tc * TCH + TCH],
                          start=(i == 0), stop=(i == NC - 1))
                em.act(qt, ps, AF.Copy)
                for j in range(TCH // 128):
                    em.transpose(pst, qt[:, j * 128:(j + 1) * 128], idb)
                    em.copy(vev, pst)
                    em.store(vtm_d[tc * TCH + j * 128: tc * TCH + (j + 1) * 128,
                                   o * 128:(o + 1) * 128], vev)
    with Phase(em) as ph:
        kc = ph.sbuf("f3kc", [128, 2 * T], BF16)
        for i in range(2):
            em.load(kc[:, i * T:(i + 1) * T], kb_d[i * 128:(i + 1) * 128, :])
        sqv = ph.sbuf("f3sq", [128, TCH], F32)
        nrm2 = ph.sbuf("f3nrm2", [1, T], F32)
        kmax = ph.sbuf("f3kmax", [1, 1], F32)
        tmp1 = ph.sbuf("f3tmp1", [1, T], F32)
        psr = ph.psum([1, TCH])
        for tc in range(NT):
            for i in range(2):
                em.tt(sqv, kc[:, i * T + tc * TCH: i * T + tc * TCH + TCH],
                      kc[:, i * T + tc * TCH: i * T + tc * TCH + TCH], ALU.mult)
                em.mm(psr, c["ones"][0:128, 0:1], sqv, start=(i == 0), stop=(i == 1))
            em.act(tmp1[0:1, tc * TCH:(tc + 1) * TCH], psr[0:1, :], AF.Sqrt)
        em.red(kmax, tmp1, ALU.max)
        qh = ph.sbuf("f3qh", [128, 2 * T], BF16)
        ex = ph.sbuf("f3ex", [128, TCH], BF16)
        den = ph.sbuf("f3den", [1, TCH], F32)
        rden = ph.sbuf("f3rden", [1, TCH], F32)
        negm = ph.sbuf("f3negm", [1, T], F32)
        oacc = ph.sbuf("f3oacc", [128, TCH], F32)
        vtm = ph.sbuf("f3vtm", [128, 256], BF16)
        psc = ph.psum([128, TCH])
        pso1 = ph.psum([128, TCH])
        pso2 = ph.psum([128, TCH])
        psd = ph.psum([1, TCH])
        psb = ph.psum([128, TCH])
        for h in range(H):
            for i in range(2):
                em.load(qh[:, i * T:(i + 1) * T],
                        qb_d[(h * 2 + i) * 128:(h * 2 + i + 1) * 128, :])
            for tc in range(NT):
                for i in range(2):
                    em.tt(sqv, qh[:, i * T + tc * TCH: i * T + tc * TCH + TCH],
                          qh[:, i * T + tc * TCH: i * T + tc * TCH + TCH], ALU.mult)
                    em.mm(psr, c["ones"][0:128, 0:1], sqv, start=(i == 0), stop=(i == 1))
                em.act(nrm2[0:1, tc * TCH:(tc + 1) * TCH], psr[0:1, :], AF.Sqrt)
            em.ts(negm, nrm2, kmax[0:1, 0:1], ALU.mult)
            em.ts(negm, negm, -1.0, ALU.mult)
            for tqc in range(NT):
                first = True
                for tk in range(T // 128):
                    em.mm(psc, kc[:, tk * 128: tk * 128 + 128],
                          qh[:, tqc * TCH: tqc * TCH + TCH], start=True, stop=False)
                    em.mm(psc, kc[:, T + tk * 128: T + tk * 128 + 128],
                          qh[:, T + tqc * TCH: T + tqc * TCH + TCH], start=False, stop=False)
                    em.mm(psc, c["ones1"][0:1, 0:128], negm[0:1, tqc * TCH:(tqc + 1) * TCH],
                          start=False, stop=True)
                    em.act(ex, psc, AF.Exp, scale=1.0 / 16.0)
                    em.load(vtm, vtm_d[tk * 128:(tk + 1) * 128, :])
                    em.mm(pso1, vtm[:, 0:128], ex, start=first, stop=(tk == T // 128 - 1))
                    em.mm(pso2, vtm[:, 128:256], ex, start=first, stop=(tk == T // 128 - 1))
                    em.mm(psd, c["ones_bf"][0:128, 0:1], ex, start=first, stop=(tk == T // 128 - 1))
                    first = False
                em.copy(den, psd[0:1, :])
                em.recip(rden, den)
                em.mm(psb, c["ones"][0:1, 0:128], rden)
                for i, pso in ((0, pso1), (1, pso2)):
                    em.copy(oacc, pso)
                    em.tt(oacc, oacc, psb, ALU.mult)
                    em.store(x2_f[(h * 2 + i) * 128:(h * 2 + i + 1) * 128,
                                  tqc * TCH:(tqc + 1) * TCH], oacc)
    with Phase(em) as ph:
        ob = ph.sbuf("f3ob", [128, NC * T], BF16)
        st = ph.sbuf("f3st", [128, T], F32)
        for i in range(NC):
            em.load(st, x2_f[i * 128:(i + 1) * 128, :])
            em.act(ob[:, i * T:(i + 1) * T], st, AF.Copy)
        wo = load_weight_bf(em, ph, "f3wo", WoT, D, D)
        ps = ph.psum([128, TCH])
        ta = ph.sbuf("f3ta", [128, TCH], F32)
        for tc in range(NT):
            for o in range(NC):
                for i in range(NC):
                    em.mm(ps, wo.chunk(i, o * 128, (o + 1) * 128),
                          ob[:, i * T + tc * TCH: i * T + tc * TCH + TCH],
                          start=(i == 0), stop=(i == NC - 1))
                em.load(ta, xn[o * 128:(o + 1) * 128, tc * TCH:(tc + 1) * TCH])
                em.tt(ta, ta, ps, ALU.add)
                em.store(x2_f[o * 128:(o + 1) * 128, tc * TCH:(tc + 1) * TCH], ta)
    rms_fm(em, c, x2_f, nx_f, nx_b, nx_l, eps=1e-6, gain=g2)
    with Phase(em) as ph:
        nxb = ph.sbuf("f3nxb2", [128, NC * T], BF16)
        for i in range(NC):
            em.load(nxb[:, i * T:(i + 1) * T], nx_b[i * 128:(i + 1) * 128, :])
        psl = ph.psum([128, E])
        pst = ph.psum([E, 128])
        topk_rows(em, c, ph, nxb, nx_l, WrT, ebias, E, crows_f32, True, psl, pst, "f3r",
                  rows_bf_dram=crows)
        w1 = load_weight_bf(em, ph, "f3sw1", sW1T, D, D)
        w3 = load_weight_bf(em, ph, "f3sw3", sW3T, D, D)
        w2 = load_weight_bf(em, ph, "f3sw2", sW2T, D, D)
        shb = ph.sbuf("f3shb", [128, NC * TCH], BF16)
        ps = ph.psum([128, TCH])
        ps2 = ph.psum([128, TCH])
        ta = ph.sbuf("f3ta2", [128, TCH], F32)
        for tc in range(NT):
            for o in range(NC):
                for i in range(NC):
                    em.mm(ps, w1.chunk(i, o * 128, (o + 1) * 128),
                          nxb[:, i * T + tc * TCH: i * T + tc * TCH + TCH],
                          start=(i == 0), stop=(i == NC - 1))
                for i in range(NC):
                    em.mm(ps2, w3.chunk(i, o * 128, (o + 1) * 128),
                          nxb[:, i * T + tc * TCH: i * T + tc * TCH + TCH],
                          start=(i == 0), stop=(i == NC - 1))
                em.act(ta, ps, AF.Silu)
                em.tt(ta, ta, ps2, ALU.mult)
                em.act(shb[:, o * TCH:(o + 1) * TCH], ta, AF.Copy)
            for o in range(NC):
                for i in range(NC):
                    em.mm(ps, w2.chunk(i, o * 128, (o + 1) * 128),
                          shb[:, i * TCH:(i + 1) * TCH], start=(i == 0), stop=(i == NC - 1))
                em.copy(ta, ps)
                em.store(sh_d[o * 128:(o + 1) * 128, tc * TCH:(tc + 1) * TCH], ta)
        for e in range(E):
            cast_dram_bf(em, ph, eW1T[e], wbf[(e * 3 + 0) * 1024:(e * 3 + 1) * 1024, :], D, D, "ce1_%d" % e)
            cast_dram_bf(em, ph, eW3T[e], wbf[(e * 3 + 1) * 1024:(e * 3 + 2) * 1024, :], D, D, "ce3_%d" % e)
            cast_dram_bf(em, ph, eW2T[e], wbf[(e * 3 + 2) * 1024:(e * 3 + 3) * 1024, :], D, D, "ce2_%d" % e)
    with Phase(em) as ph:
        nxb = ph.sbuf("f3nxb3", [128, NC * T], BF16)
        for i in range(NC):
            em.load(nxb[:, i * T:(i + 1) * T], nx_b[i * 128:(i + 1) * 128, :])
        pool = TilePool(ph, "f3w", 8, [128, 512], BF16)
        hall = ph.sbuf("f3hall", [128, E * NC * TCH], BF16)
        crow = ph.sbuf("f3crow", [1, T], BF16)
        ps = ph.psum([128, TCH])
        ps2 = ph.psum([128, TCH])
        psb = ph.psum([128, TCH])
        ta = ph.sbuf("f3ta3", [128, TCH], F32)
        tb = ph.sbuf("f3tb3", [128, TCH], BF16)
        tx = ph.sbuf("f3tx", [128, TCH], F32)
        w1rows = [ph.sbuf("f3w1r%d" % i, [128, 1024], BF16) for i in range(NC)]
        w3rows = [ph.sbuf("f3w3r%d" % i, [128, 1024], BF16) for i in range(NC)]
        ps4 = [ph.psum([128, TCH]) for _ in range(4)]
        for tc in range(NT):
            for e in range(E):
                em.load(crow, crows[e:e + 1, :])
                em.mm(psb, c["ones_bf"][0:1, 0:128], crow[0:1, tc * TCH:(tc + 1) * TCH])
                em.copy(tb, psb)
                for i in range(NC):
                    em.load(w1rows[i], wbf[(e * 3 + 0) * 1024 + i * 128:(e * 3 + 0) * 1024 + (i + 1) * 128, :])
                    em.load(w3rows[i], wbf[(e * 3 + 1) * 1024 + i * 128:(e * 3 + 1) * 1024 + (i + 1) * 128, :])
                for f in range(NC):
                    for i in range(NC):
                        em.mm(ps, w1rows[i][:, f * 128:(f + 1) * 128],
                              nxb[:, i * T + tc * TCH: i * T + tc * TCH + TCH],
                              start=(i == 0), stop=(i == NC - 1))
                    for i in range(NC):
                        em.mm(ps2, w3rows[i][:, f * 128:(f + 1) * 128],
                              nxb[:, i * T + tc * TCH: i * T + tc * TCH + TCH],
                              start=(i == 0), stop=(i == NC - 1))
                    em.act(ta, ps, AF.Silu)
                    em.tt(ta, ta, ps2, ALU.mult)
                    em.tt(hall[:, (e * NC + f) * TCH:(e * NC + f + 1) * TCH], ta, tb, ALU.mult)
            for rnd in range(2):
                for e in range(E):
                    for f in range(NC):
                        w1t = pool.next()
                        em.load(w1t[:, 0:512],
                                wbf[(e * 3 + 2) * 1024 + f * 128:(e * 3 + 2) * 1024 + (f + 1) * 128,
                                    rnd * 512:(rnd + 1) * 512])
                        for o4 in range(4):
                            em.mm(ps4[o4], w1t[:, o4 * 128:o4 * 128 + 128],
                                  hall[:, (e * NC + f) * TCH:(e * NC + f + 1) * TCH],
                                  start=(e == 0 and f == 0), stop=(e == E - 1 and f == NC - 1))
                for o4 in range(4):
                    o = rnd * 4 + o4
                    em.load(ta, sh_d[o * 128:(o + 1) * 128, tc * TCH:(tc + 1) * TCH])
                    em.load(tx, x2_f[o * 128:(o + 1) * 128, tc * TCH:(tc + 1) * TCH])
                    em.tt(ta, ta, ps4[o4], ALU.add)
                    em.tt(ta, ta, tx, ALU.add)
                    em.store(out[o * 128:(o + 1) * 128, tc * TCH:(tc + 1) * TCH], ta)



def build_l2_attn(em, c, io):
    xn_b = io["xn_bf"]; out = io["out_fm"]
    Wq, Wk, Wv, Wo = io["l2a_Wq"], io["l2a_Wk"], io["l2a_Wv"], io["l2a_Wo"]
    q_d = io["l2a_q_scratch"]
    k_d = io["l2a_k_scratch"]
    ktm_d = io["l2a_ktm_scratch"]
    vtm_d = io["l2a_vtm_scratch"]
    o_d = io["l2a_o_scratch"]
    NH = 16
    DK = 64
    EPS = 1e-6
    with Phase(em) as ph:
        xnb = ph.sbuf("a2xnb", [128, NC * T], BF16)
        for i in range(NC):
            em.load(xnb[:, i * T:(i + 1) * T], xn_b[i * 128:(i + 1) * 128, :])
        wq = load_weight_bf(em, ph, "a2wq", Wq, D, D)
        wk = load_weight_bf(em, ph, "a2wk", Wk, D, D)
        wv = load_weight_bf(em, ph, "a2wv", Wv, D, D)
        ps = ph.psum([128, TCH])
        pst = ph.psum([128, 128], BF16)
        t1 = ph.sbuf("a2t1", [128, TCH], F32)
        t2 = ph.sbuf("a2t2", [128, TCH], F32)
        qt = ph.sbuf("a2qt", [128, TCH], BF16)
        vev = ph.sbuf("a2vev", [128, 128], BF16)
        idb = ph.sbuf("a2idb", [128, 128], BF16)
        em.act(idb, c["ident"], AF.Copy)
        for tc in range(NT):
            for o in range(NC):
                for w_, dst, dotm in ((wq, q_d, False), (wk, k_d, True)):
                    for i in range(NC):
                        em.mm(ps, w_.chunk(i, o * 128, (o + 1) * 128),
                              xnb[:, i * T + tc * TCH: i * T + tc * TCH + TCH],
                              start=(i == 0), stop=(i == NC - 1))
                    em.ts(t1, ps, 0.0, ALU.min)
                    em.act(t1, t1, AF.Exp)
                    em.ts(t2, ps, 0.0, ALU.max)
                    em.tt(qt, t1, t2, ALU.add)
                    em.store(dst[o * 128:(o + 1) * 128, tc * TCH:(tc + 1) * TCH], qt)
                    if dotm:
                        for j in range(TCH // 128):
                            em.transpose(pst, qt[:, j * 128:(j + 1) * 128], idb)
                            em.copy(vev, pst)
                            em.store(ktm_d[tc * TCH + j * 128: tc * TCH + (j + 1) * 128,
                                           o * 128:(o + 1) * 128], vev)
                for i in range(NC):
                    em.mm(ps, wv.chunk(i, o * 128, (o + 1) * 128),
                          xnb[:, i * T + tc * TCH: i * T + tc * TCH + TCH],
                          start=(i == 0), stop=(i == NC - 1))
                em.act(qt, ps, AF.Copy)
                for j in range(TCH // 128):
                    em.transpose(pst, qt[:, j * 128:(j + 1) * 128], idb)
                    em.copy(vev, pst)
                    em.store(vtm_d[tc * TCH + j * 128: tc * TCH + (j + 1) * 128,
                                   o * 128:(o + 1) * 128], vev)
    with Phase(em) as ph:
        CH = 128
        NCHUNK = T // CH
        qf = ph.sbuf("a2qf", [64, T], BF16)
        kf = ph.sbuf("a2kf", [64, T], BF16)
        ktm = ph.sbuf("a2ktm", [128, 64], BF16)
        vtm = ph.sbuf("a2vtm", [128, 64], BF16)
        S = ph.sbuf("a2S", [64, 64], F32)
        Sb = ph.sbuf("a2Sb", [64, 64], BF16)
        z = ph.sbuf("a2z", [64, 1], F32)
        scm = ph.sbuf("a2scm", [128, 128], BF16)
        tsq = ph.sbuf("a2tsq", [64, 128], F32)
        rowd = ph.sbuf("a2rowd", [1, 128], F32)
        rrow = ph.sbuf("a2rrow", [1, 128], F32)
        ot = ph.sbuf("a2ot", [64, 128], F32)
        psc = ph.psum([128, 128])
        pso = ph.psum([64, 128])
        psden = ph.psum([1, 128])
        psS = ph.psum([64, 64])
        psb = ph.psum([64, 128])
        for h in range(NH):
            em.load(qf, q_d[h * 64:(h + 1) * 64, :])
            em.load(kf, k_d[h * 64:(h + 1) * 64, :])
            em.memset(S, 0.0, eng="vector")
            em.memset(z, 0.0, eng="vector")
            for ch in range(NCHUNK):
                em.load(ktm, ktm_d[ch * CH:(ch + 1) * CH, h * 64:(h + 1) * 64])
                em.load(vtm, vtm_d[ch * CH:(ch + 1) * CH, h * 64:(h + 1) * 64])
                qc = qf[:, ch * CH:(ch + 1) * CH]
                kc_ = kf[:, ch * CH:(ch + 1) * CH]
                em.mm(psc, kc_, qc)
                em.tt(scm, psc, c["triui"], ALU.mult)
                em.mm(pso, vtm, scm, start=True, stop=False)
                em.act(Sb, S, AF.Copy)
                em.mm(pso, Sb, qc, start=False, stop=True)
                em.mm(psden, c["ones_bf"][0:128, 0:1], scm, start=True, stop=False)
                em.ts(tsq, qc, z[:, 0:1], ALU.mult)
                em.mm(psden, c["ones"][0:64, 0:1], tsq, start=False, stop=True)
                em.copy(rowd, psden)
                em.ts(rowd, rowd, float(EPS), ALU.add)
                em.recip(rrow, rowd)
                em.mm(psb, c["ones"][0:1, 0:64], rrow)
                em.copy(ot, pso)
                em.tt(ot, ot, psb, ALU.mult)
                em.store(o_d[h * 64:(h + 1) * 64, ch * CH:(ch + 1) * CH], ot)
                em.mm(psS, ktm, vtm)
                em.tt(S, S, psS, ALU.add)
                em.red(tsq[:, 0:1], kc_, ALU.add)
                em.tt(z, z, tsq[:, 0:1], ALU.add)
    with Phase(em) as ph:
        ob = ph.sbuf("a2ob", [128, NC * T], BF16)
        for i in range(NC):
            em.load(ob[:, i * T:(i + 1) * T], o_d[i * 128:(i + 1) * 128, :])
        wo = load_weight_bf(em, ph, "a2wo", Wo, D, D)
        ps = ph.psum([128, TCH])
        ta = ph.sbuf("a2ta", [128, TCH], F32)
        for tc in range(NT):
            for o in range(NC):
                for i in range(NC):
                    em.mm(ps, wo.chunk(i, o * 128, (o + 1) * 128),
                          ob[:, i * T + tc * TCH: i * T + tc * TCH + TCH],
                          start=(i == 0), stop=(i == NC - 1))
                em.copy(ta, ps)
                em.store(out[o * 128:(o + 1) * 128, tc * TCH:(tc + 1) * TCH], ta)



def _dwconv_silu(em, c, ph, ps, wv_, xnb, convw, dst_d, tcol, t1, t2, do_silu=True, K_=4):
    for o in range(NC):
        em.load(tcol, convw[o * 128:(o + 1) * 128, :])
        for tc in range(NT):
            for i in range(NC):
                em.mm(ps, wv_.chunk(i, o * 128, (o + 1) * 128),
                      xnb[:, i * T + tc * TCH: i * T + tc * TCH + TCH],
                      start=(i == 0), stop=(i == NC - 1))
            em.copy(t1[:, tc * TCH:(tc + 1) * TCH], ps)
        em.ts(t2, t1, tcol[:, K_ - 1:K_], ALU.mult)
        for k_ in range(K_ - 1):
            sh = K_ - 1 - k_
            em.stt(t2[:, sh:T], t1[:, 0:T - sh], tcol[:, k_:k_ + 1], t2[:, sh:T],
                   ALU.mult, ALU.add)
        if do_silu:
            em.act(t1, t2, AF.Silu)
        else:
            em.copy(t1, t2)
        em.act(tbf := t2, t1, AF.Copy) if False else None
        em.store(dst_d[o * 128:(o + 1) * 128, :], t1)


def build_l1_attn(em, c, io):
    xnb_d = io["xn_bf"]; out = io["out_fm"]
    Wq, Wk, Wv, Wb = io["l1a_Wq"], io["l1a_Wk"], io["l1a_Wv"], io["l1a_Wb"]
    cq, ck, cv = io["l1a_conv_q"], io["l1a_conv_k"], io["l1a_conv_v"]
    firS, firL = io["l1a_fir_short"], io["l1a_fir_long"]
    Wf1, bf1, Wf2, bf2 = io["l1a_Wf1"], io["l1a_bf1"], io["l1a_Wf2"], io["l1a_bf2"]
    rmsw = io["l1a_rms_w"]
    Wo = io["l1a_Wo"]
    q_d = io["l1a_q_d"]; k_d = io["l1a_k_d"]; v_d = io["l1a_v_d"]
    qn_d = io["l1a_qn_d"]; kn_d = io["l1a_kn_d"]
    beta_d = io["l1a_beta_d"]
    fw_d = io["l1a_fw_d"]
    delta_d = io["l1a_delta_d"]
    comb_d = io["l1a_comb_d"]
    NH = 4
    DK = 256
    with Phase(em) as ph:
        xnb = ph.sbuf("a1xnb", [128, NC * T], BF16)
        for i in range(NC):
            em.load(xnb[:, i * T:(i + 1) * T], xnb_d[i * 128:(i + 1) * 128, :])
        ps = ph.psum([128, TCH])
        t1 = ph.sbuf("a1t1", [128, T], F32)
        t2 = ph.sbuf("a1t2", [128, T], F32)
        tcol = ph.sbuf("a1tcol", [128, 4], F32)
        wqkv = [load_weight_bf(em, ph, "a1w%d" % j, W_, D, D)
                for j, W_ in enumerate((Wq, Wk, Wv))]
        for wv_, cw, dst in zip(wqkv, (cq, ck, cv), (q_d, k_d, v_d)):
            _dwconv_silu(em, c, ph, ps, wv_, xnb, cw, dst, tcol, t1, t2)
        wbt = ph.sbuf("a1wbt", [128, NC * 4], BF16)
        stg4 = ph.sbuf("a1stg4", [128, 4], F32)
        for i in range(NC):
            em.load(stg4, Wb[i * 128:(i + 1) * 128, :])
            em.act(wbt[:, i * 4:(i + 1) * 4], stg4, AF.Copy)
        psl = ph.psum([128, 4])
        pst4 = ph.psum([4, 128])
        bt = ph.sbuf("a1bt", [128, 4], F32)
        btT = ph.sbuf("a1btT", [4, 128], F32)
        for tt_ in range(T // 128):
            for i in range(NC):
                em.mm(psl, xnb[:, i * T + tt_ * 128: i * T + tt_ * 128 + 128],
                      wbt[:, i * 4:(i + 1) * 4], start=(i == 0), stop=(i == NC - 1))
            em.act(bt, psl, AF.Sigmoid)
            em.transpose(pst4, bt, c["ident"])
            em.copy(btT, pst4)
            em.store(beta_d[:, tt_ * 128:(tt_ + 1) * 128], btT)
    with Phase(em) as ph:
        xnb = ph.sbuf("a1xnb2", [128, NC * T], BF16)
        for i in range(NC):
            em.load(xnb[:, i * T:(i + 1) * T], xnb_d[i * 128:(i + 1) * 128, :])
        ps = ph.psum([128, TCH])
        t1 = ph.sbuf("a1t1b", [128, T], F32)
        t2 = ph.sbuf("a1t2b", [128, T], F32)
        wf1 = load_weight_bf(em, ph, "a1wf1", Wf1, D, 2048)
        h1b = ph.sbuf("a1h1b", [128, 16 * TCH], BF16)
        bcol = ph.sbuf("a1bcol", [128, 1], F32)
        wf2 = ph.sbuf("a1wf2", [128, 16 * 16], BF16)
        stg16 = ph.sbuf("a1stg16", [128, 16], F32)
        for i in range(16):
            em.load(stg16, Wf2[i * 128:(i + 1) * 128, :])
            em.act(wf2[:, i * 16:(i + 1) * 16], stg16, AF.Copy)
        brow = ph.sbuf("a1brow", [1, 16], F32)
        em.load(brow, bf2.rearrange("(a b) -> a b", a=1))
        psf = ph.psum([128, 16])
        pst16 = ph.psum([16, 128])
        fwt = ph.sbuf("a1fwt", [128, 16], F32)
        fv1 = ph.sbuf("a1fv1", [128, 4], F32)
        fex = ph.sbuf("a1fex", [128, 16], F32)
        fwT = ph.sbuf("a1fwT", [16, 128], F32)
        for tc in range(NT):
            for o in range(16):
                em.load(bcol, bf1[o * 128:(o + 1) * 128].rearrange("(a b) -> a b", b=1))
                for i in range(NC):
                    em.mm(ps, wf1.chunk(i, o * 128, (o + 1) * 128),
                          xnb[:, i * T + tc * TCH: i * T + tc * TCH + TCH],
                          start=(i == 0), stop=(i == NC - 1))
                em.act(t1[:, 0:TCH], ps, AF.Gelu, bias=bcol[:, 0:1])
                em.act(h1b[:, o * TCH:(o + 1) * TCH], t1[:, 0:TCH], AF.Copy)
            for tt_ in range(TCH // 128):
                for i in range(16):
                    em.mm(psf, h1b[:, i * TCH + tt_ * 128: i * TCH + tt_ * 128 + 128],
                          wf2[:, i * 16:(i + 1) * 16], start=(i == 0), stop=False)
                em.mm(psf, c["ones"][0:1, 0:128], brow, start=False, stop=True)
                em.copy(fwt, psf)
                fg = fwt.rearrange("p (h j) -> p h j", h=4)
                em.red(fv1, fg, ALU.max)
                em.ts(fv1, fv1, -1.0, ALU.mult)
                for hh in range(4):
                    em.act(fex[:, hh * 4:(hh + 1) * 4], fwt[:, hh * 4:(hh + 1) * 4],
                           AF.Exp, bias=fv1[:, hh:hh + 1])
                em.red(fv1, fex.rearrange("p (h j) -> p h j", h=4), ALU.add)
                em.recip(fv1, fv1) if False else None
                em.recip(fv1_r := fwt[:, 0:4], fv1)
                for hh in range(4):
                    em.ts(fex[:, hh * 4:(hh + 1) * 4], fex[:, hh * 4:(hh + 1) * 4],
                          fv1_r[:, hh:hh + 1], ALU.mult)
                em.transpose(pst16, fex, c["ident"])
                em.copy(fwT, pst16)
                em.store(fw_d[:, tc * TCH + tt_ * 128: tc * TCH + (tt_ + 1) * 128], fwT)
        psn = ph.psum([1, TCH])
        psnb = ph.psum([128, TCH])
        nrow = ph.sbuf("a1nrow", [1, TCH], F32)
        for src, dst in ((q_d, qn_d), (k_d, kn_d)):
            for h in range(NH):
                for tc in range(NT):
                    for i in range(2):
                        r = h * 2 + i
                        em.load(t1[:, 0:TCH], src[r * 128:(r + 1) * 128, tc * TCH:(tc + 1) * TCH])
                        em.tt(t2[:, i * TCH:i * TCH + TCH], t1[:, 0:TCH], t1[:, 0:TCH], ALU.mult)
                        em.mm(psn, c["ones"][0:128, 0:1], t2[:, i * TCH:i * TCH + TCH],
                              start=(i == 0), stop=(i == 1))
                    em.act(nrow, psn[0:1, :], AF.Sqrt)
                    em.recip(nrow, nrow) if False else None
                    em.recip(nrow2 := t2[0:1, 0:TCH], nrow)
                    em.mm(psnb, c["ones"][0:1, 0:128], nrow2)
                    for i in range(2):
                        r = h * 2 + i
                        em.load(t1[:, 0:TCH], src[r * 128:(r + 1) * 128, tc * TCH:(tc + 1) * TCH])
                        em.tt(t1[:, TCH:2 * TCH], t1[:, 0:TCH], psnb, ALU.mult)
                        em.act(tbf2 := h1b[:, 0:TCH], t1[:, TCH:2 * TCH], AF.Copy)
                        em.store(dst[r * 128:(r + 1) * 128, tc * TCH:(tc + 1) * TCH],
                                 h1b[:, 0:TCH])
    with Phase(em) as ph:
        CH = 128
        qn = ph.sbuf("d1qn", [128, 2 * T], BF16)
        kn = ph.sbuf("d1kn", [128, 2 * T], BF16)
        vb = ph.sbuf("d1vb", [128, 2 * T], BF16)
        kb = ph.sbuf("d1kb", [128, 2 * T], BF16)
        brow_ = ph.sbuf("d1brow", [1, T], F32)
        S0 = ph.sbuf("d1S0", [128, 256], F32)
        S1 = ph.sbuf("d1S1", [128, 256], F32)
        S0b = ph.sbuf("d1S0b", [128, 256], BF16)
        S1b = ph.sbuf("d1S1b", [128, 256], BF16)
        idb = ph.sbuf("d1idb", [128, 128], BF16)
        em.act(idb, c["ident"], AF.Copy)
        trb = ph.sbuf("d1trb", [128, 128], BF16)
        em.act(trb, c["trils"], AF.Copy)
        tub = ph.sbuf("d1tub", [128, 128], BF16)
        em.act(tub, c["triui"], AF.Copy)
        t1 = ph.sbuf("d1t1", [128, TCH], F32)
        A = ph.sbuf("d1A", [128, 128], BF16)
        AT = ph.sbuf("d1AT", [128, 128], BF16)
        G = ph.sbuf("d1G", [128, 128], BF16)
        GT = ph.sbuf("d1GT", [128, 128], BF16)
        P2 = ph.sbuf("d1P2", [128, 128], BF16)
        PT2 = ph.sbuf("d1PT2", [128, 128], BF16)
        attnT = ph.sbuf("d1attnT", [128, 128], BF16)
        vtm = ph.sbuf("d1vtm", [128, 256], BF16)
        ktm = ph.sbuf("d1ktm", [128, 256], BF16)
        kbtm = ph.sbuf("d1kbtm", [128, 256], BF16)
        ub = ph.sbuf("d1ub", [128, 256], BF16)
        utb = ph.sbuf("d1utb", [128, 256], BF16)
        wfm = ph.sbuf("d1wfm", [128, 2 * 128], BF16)
        psA = ph.psum([128, 128])
        psB = ph.psum([128, 128])
        psBc = ph.psum([128, TCH])
        pstb = ph.psum([128, 128], BF16)
        psO = ph.psum([128, 256])
        psW = ph.psum([128, 256])
        for h in range(NH):
            em.load(brow_, beta_d[h:h + 1, :])
            for i in range(2):
                r = h * 2 + i
                em.load(qn[:, i * T:(i + 1) * T], qn_d[r * 128:(r + 1) * 128, :])
                em.load(kn[:, i * T:(i + 1) * T], kn_d[r * 128:(r + 1) * 128, :])
                for tc in range(NT):
                    em.load(t1, v_d[r * 128:(r + 1) * 128, tc * TCH:(tc + 1) * TCH])
                    em.mm(psBc, c["ones"][0:1, 0:128], brow_[0:1, tc * TCH:(tc + 1) * TCH])
                    em.tt(vb[:, i * T + tc * TCH: i * T + tc * TCH + TCH], t1, psBc, ALU.mult)
                    em.tt(kb[:, i * T + tc * TCH: i * T + tc * TCH + TCH],
                          kn[:, i * T + tc * TCH: i * T + tc * TCH + TCH], psBc, ALU.mult)
            em.memset(S0, 0.0, eng="vector")
            em.memset(S1, 0.0, eng="vector")
            for nchk in range(T // CH):
                sl = slice(nchk * CH, (nchk + 1) * CH)
                qc = [qn[:, i * T + nchk * CH: i * T + nchk * CH + CH] for i in range(2)]
                kc_ = [kn[:, i * T + nchk * CH: i * T + nchk * CH + CH] for i in range(2)]
                kbc = [kb[:, i * T + nchk * CH: i * T + nchk * CH + CH] for i in range(2)]
                vbc = [vb[:, i * T + nchk * CH: i * T + nchk * CH + CH] for i in range(2)]
                for i in range(2):
                    em.transpose(pstb, vbc[i], idb)
                    em.copy(vtm[:, i * 128:(i + 1) * 128], pstb)
                    em.transpose(pstb, kc_[i], idb)
                    em.copy(ktm[:, i * 128:(i + 1) * 128], pstb)
                    em.transpose(pstb, kbc[i], idb)
                    em.copy(kbtm[:, i * 128:(i + 1) * 128], pstb)
                for i in range(2):
                    em.mm(psA, kbc[i], kc_[i], start=(i == 0), stop=(i == 1))
                em.stt(A, psA, -1.0, trb, ALU.mult, ALU.mult)
                for i in range(2):
                    em.mm(psA, kc_[i], kbc[i], start=(i == 0), stop=(i == 1))
                em.stt(AT, psA, -1.0, c["trius"], ALU.mult, ALU.mult)
                em.tt(G, idb, A, ALU.add)
                em.tt(GT, idb, AT, ALU.add)
                for _ in range(6):
                    em.mm(psA, AT, A)
                    em.mm(psB, A, AT)
                    em.copy(P2, psA)
                    em.copy(PT2, psB)
                    em.mm(psA, GT, P2)
                    em.mm(psB, P2, GT)
                    em.tt(G, G, psA, ALU.add)
                    em.tt(GT, GT, psB, ALU.add)
                    em.copy(A, P2)
                    em.copy(AT, PT2)
                em.mm(psO, GT, vtm)
                em.copy(ub, psO)
                for i in range(2):
                    em.mm(psW[:, 0:128], kbtm[:, i * 128:(i + 1) * 128], GT, start=True, stop=True)
                    em.copy(wfm[:, i * 128:(i + 1) * 128], psW[:, 0:128])
                for i in range(2):
                    em.mm(psA, kc_[i], qc[i], start=(i == 0), stop=(i == 1))
                em.tt(attnT, psA, tub, ALU.mult)
                em.act(S0b, S0, AF.Copy)
                em.act(S1b, S1, AF.Copy)
                for i in range(2):
                    em.mm(psO, wfm[:, i * 128:(i + 1) * 128], (S0b, S1b)[i],
                          start=(i == 0), stop=(i == 1))
                em.tt(utb, ub, psO, ALU.subtract)
                for i in range(2):
                    em.mm(psO, qc[i], (S0b, S1b)[i], start=(i == 0), stop=False)
                em.mm(psO, attnT, utb, start=False, stop=True)
                em.copy(ub, psO)
                for i in range(2):
                    em.transpose(pstb, ub[:, i * 128:(i + 1) * 128], idb)
                    em.copy(A, pstb)
                    em.store(delta_d[(h * 2 + i) * 128:(h * 2 + i + 1) * 128, sl], A)
                for i in range(2):
                    em.mm(psW, ktm[:, i * 128:(i + 1) * 128], utb)
                    em.tt((S0, S1)[i], (S0, S1)[i], psW, ALU.add)
    with Phase(em) as ph:
        ps = ph.psum([128, TCH])
        psb = ph.psum([128, TCH])
        psn = ph.psum([1, TCH])
        t1 = ph.sbuf("c1t1", [128, T], F32)
        t2 = ph.sbuf("c1t2", [128, T], F32)
        acc = ph.sbuf("c1acc", [128, T], F32)
        fS = ph.sbuf("c1fS", [128, 7], F32)
        fL = ph.sbuf("c1fL", [128, 64], F32)
        frow = ph.sbuf("c1frow", [1, T], F32)
        rcol = ph.sbuf("c1rcol", [128, 1], F32)
        nrow = ph.sbuf("c1nrow", [1, TCH], F32)
        nrow2 = ph.sbuf("c1nrow2", [1, TCH], F32)
        obf = ph.sbuf("c1obf", [128, NC * T], BF16)
        for o in range(NC):
            h = o // 2
            em.load(t1, v_d[o * 128:(o + 1) * 128, :])
            em.load(frow, fw_d[h * 4 + 3:h * 4 + 4, :])
            for tc in range(NT):
                em.mm(psb, c["ones"][0:1, 0:128], frow[0:1, tc * TCH:(tc + 1) * TCH])
                em.tt(acc[:, tc * TCH:(tc + 1) * TCH], t1[:, tc * TCH:(tc + 1) * TCH], psb, ALU.mult)
            em.load(t2, delta_d[o * 128:(o + 1) * 128, :])
            em.load(frow, fw_d[h * 4 + 2:h * 4 + 3, :])
            for tc in range(NT):
                em.mm(psb, c["ones"][0:1, 0:128], frow[0:1, tc * TCH:(tc + 1) * TCH])
                em.tt(t2[:, tc * TCH:(tc + 1) * TCH], t2[:, tc * TCH:(tc + 1) * TCH], psb, ALU.mult)
            em.tt(acc, acc, t2, ALU.add)
            em.load(fS, firS[o * 128:(o + 1) * 128, :])
            em.load(fL, firL[o * 128:(o + 1) * 128, :])
            for filt, K_, fwj in ((fS, 7, 0), (fL, 64, 1)):
                em.ts(t2, t1, filt[:, K_ - 1:K_], ALU.mult)
                for k_ in range(K_ - 1):
                    sh = K_ - 1 - k_
                    em.stt(t2[:, sh:T], t1[:, 0:T - sh], filt[:, k_:k_ + 1], t2[:, sh:T],
                           ALU.mult, ALU.add)
                em.load(frow, fw_d[h * 4 + fwj:h * 4 + fwj + 1, :])
                for tc in range(NT):
                    em.mm(psb, c["ones"][0:1, 0:128], frow[0:1, tc * TCH:(tc + 1) * TCH])
                    em.tt(t2[:, tc * TCH:(tc + 1) * TCH], t2[:, tc * TCH:(tc + 1) * TCH], psb, ALU.mult)
                em.tt(acc, acc, t2, ALU.add)
            em.store(comb_d[o * 128:(o + 1) * 128, :], acc)
        for h in range(NH):
            for tc in range(NT):
                for i in range(2):
                    r = h * 2 + i
                    em.load(t1[:, 0:TCH], comb_d[r * 128:(r + 1) * 128, tc * TCH:(tc + 1) * TCH])
                    em.tt(t2[:, 0:TCH], t1[:, 0:TCH], t1[:, 0:TCH], ALU.mult)
                    em.mm(psn, c["ones"][0:128, 0:1], t2[:, 0:TCH], start=(i == 0), stop=(i == 1))
                em.ts(nrow, psn[0:1, :], 1.0 / 256.0, ALU.mult, 1e-5, ALU.add)
                em.act(nrow2, nrow, AF.Sqrt)
                em.recip(nrow, nrow2)
                em.mm(psb, c["ones"][0:1, 0:128], nrow)
                for i in range(2):
                    r = h * 2 + i
                    em.load(t1[:, 0:TCH], comb_d[r * 128:(r + 1) * 128, tc * TCH:(tc + 1) * TCH])
                    em.load(rcol, rmsw[r * 128:(r + 1) * 128].rearrange("(a b) -> a b", b=1))
                    em.tt(t1[:, 0:TCH], t1[:, 0:TCH], psb, ALU.mult)
                    em.ts(t1[:, 0:TCH], t1[:, 0:TCH], rcol[:, 0:1], ALU.mult)
                    em.act(obf[:, r * T + tc * TCH: r * T + tc * TCH + TCH], t1[:, 0:TCH], AF.Copy)
        wo = load_weight_bf(em, ph, "c1wo", Wo, D, D)
        for tc in range(NT):
            for o in range(NC):
                for i in range(NC):
                    em.mm(ps, wo.chunk(i, o * 128, (o + 1) * 128),
                          obf[:, i * T + tc * TCH: i * T + tc * TCH + TCH],
                          start=(i == 0), stop=(i == NC - 1))
                em.copy(t1[:, 0:TCH], ps)
                em.store(out[o * 128:(o + 1) * 128, tc * TCH:(tc + 1) * TCH], t1[:, 0:TCH])



NL = 8
NS = 512


def build_l0_ffn_A(em, c, io, part="all"):
    if part == "back":
        _l0_expert_part(em, c, io)
        return
    xn = io["l0f_xn_fm"]
    xnb_d = io["l0f_xn_bf"]
    ipwT = io["l0f_in_proj_wT"]
    ipb = io["l0f_in_proj_b"]
    opwT = io["l0f_out_proj_wT"]
    opb = io["l0f_out_proj_b"]
    ln1g, ln1b = io["l0f_ln1_g"], io["l0f_ln1_b"]
    wg = io["l0f_wg"]
    w1 = io["l0f_w1"]
    b1 = io["l0f_b1"]
    w2 = io["l0f_w2"]
    b2 = io["l0f_b2"]
    qkv_d = io["l0f_qkv_d"]
    qtm_d = io["l0f_qtm_d"]
    ktm_d = io["l0f_ktm_d"]
    vtm_d = io["l0f_vtm_d"]
    mha_d = io["l0f_mha_d"]
    y1f_d = io["l0f_y1f_d"]
    y1b_d = io["l0f_y1b_d"]
    y1l_d = io["l0f_y1l_d"]
    y1tm_d = io["l0f_y1tm_d"]
    gates_d = io["l0f_gates_d"]
    xdisp_d = io["l0f_xdisp_d"]
    yall_d = io["l0f_yall_d"]
    ffnraw_d = io["l0f_ffnraw_d"]
    w1bf = io["l0f_w1bf"]
    w2bf = io["l0f_w2bf"]
    E = 8
    H4 = 4
    DK = 256
    with Phase(em) as ph:
        xnb = ph.sbuf("f0xnb", [128, NC * T], BF16)
        for i in range(NC):
            em.load(xnb[:, i * T:(i + 1) * T], xnb_d[i * 128:(i + 1) * 128, :])
        wip = load_weight_bf(em, ph, "f0wip", ipwT, D, 3072)
        ps = ph.psum([128, TCH])
        pstb = ph.psum([128, 128], BF16)
        bcol = ph.sbuf("f0bcol", [128, 1], F32)
        t1 = ph.sbuf("f0t1", [128, TCH], F32)
        qt = ph.sbuf("f0qt", [128, TCH], BF16)
        vev = ph.sbuf("f0vev", [128, 128], BF16)
        idb = ph.sbuf("f0idb", [128, 128], BF16)
        em.act(idb, c["ident"], AF.Copy)
        for o in range(24):
            em.load(bcol, ipb[o * 128:(o + 1) * 128].rearrange("(a b) -> a b", b=1))
            for tc in range(NT):
                for i in range(NC):
                    em.mm(ps, wip.chunk(i, o * 128, (o + 1) * 128),
                          xnb[:, i * T + tc * TCH: i * T + tc * TCH + TCH],
                          start=(i == 0), stop=(i == NC - 1))
                em.act(t1, ps, AF.Identity, bias=bcol[:, 0:1])
                em.act(qt, t1, AF.Copy)
                em.store(qkv_d[o * 128:(o + 1) * 128, tc * TCH:(tc + 1) * TCH], qt)
                part = o // 8
                r = o % 8
                dst = (qtm_d, ktm_d, vtm_d)[part]
                for j in range(TCH // 128):
                    em.transpose(pstb, qt[:, j * 128:(j + 1) * 128], idb)
                    em.copy(vev, pstb)
                    em.store(dst[tc * TCH + j * 128: tc * TCH + (j + 1) * 128,
                                 r * 128:(r + 1) * 128], vev)
    with Phase(em) as ph:
        qtm = ph.sbuf("f0qtm", [128, 1024], BF16)
        ktm = [ph.sbuf("f0ktm%d" % m, [128, 1024], BF16) for m in range(NL)]
        vtm = [ph.sbuf("f0vtm%d" % m, [128, 1024], BF16) for m in range(NL)]
        sc = ph.sbuf("f0sc", [128, 8], F32)
        t128 = ph.sbuf("f0t128", [128, DK], F32)
        v1 = ph.sbuf("f0v1", [128, 1], F32)
        oacc = ph.sbuf("f0oacc", [128, 1024], F32)
        otb = ph.sbuf("f0otb", [128, 128], BF16)
        pstb = ph.psum([128, 128], BF16)
        idb = ph.sbuf("f0idb2", [128, 128], BF16)
        em.act(idb, c["ident"], AF.Copy)
        for nt in range(NS // 128):
            for m in range(NL):
                em.load(ktm[m], ktm_d[m * NS + nt * 128: m * NS + (nt + 1) * 128, :])
                em.load(vtm[m], vtm_d[m * NS + nt * 128: m * NS + (nt + 1) * 128, :])
            for l in range(NL):
                em.load(qtm, qtm_d[l * NS + nt * 128: l * NS + (nt + 1) * 128, :])
                for h in range(H4):
                    hs = h * DK
                    for m in range(NL):
                        em.tt(t128, qtm[:, hs:hs + DK], ktm[m][:, hs:hs + DK], ALU.mult)
                        em.red(sc[:, m:m + 1], t128, ALU.add)
                    em.red(v1, sc, ALU.max)
                    em.ts(v1, v1, -1.0 / 16.0, ALU.mult)
                    em.ts(sc, sc, 1.0 / 16.0, ALU.mult)
                    em.act(sc, sc, AF.Exp, bias=v1[:, 0:1])
                    em.red(v1, sc, ALU.add)
                    em.recip(v1, v1)
                    em.ts(sc, sc, v1[:, 0:1], ALU.mult)
                    for m in range(NL):
                        if m == 0:
                            em.ts(oacc[:, hs:hs + DK], vtm[m][:, hs:hs + DK],
                                  sc[:, m:m + 1], ALU.mult)
                        else:
                            em.stt(oacc[:, hs:hs + DK], vtm[m][:, hs:hs + DK],
                                   sc[:, m:m + 1], oacc[:, hs:hs + DK], ALU.mult, ALU.add)
                for j in range(NC):
                    em.act(otb, oacc[:, j * 128:(j + 1) * 128], AF.Copy)
                    em.transpose(pstb, otb, idb)
                    em.copy(otb, pstb)
                    em.store(mha_d.rearrange("a b -> a b")[j * 128:(j + 1) * 128,
                             l * NS + nt * 128: l * NS + (nt + 1) * 128], otb)
    with Phase(em) as ph:
        ob = ph.sbuf("f0ob", [128, NC * T], BF16)
        st = ph.sbuf("f0st", [128, T], BF16)
        for i in range(NC):
            em.load(st[:, 0:T], mha_d[i * 128:(i + 1) * 128, :]) if False else None
        stf = ph.sbuf("f0stf", [128, T], F32)
        for i in range(NC):
            em.load(ob[:, i * T:(i + 1) * T], mha_d[i * 128:(i + 1) * 128, :], eng="gpsimd")
        wop = load_weight_bf(em, ph, "f0wop", opwT, D, D)
        ps = ph.psum([128, TCH])
        ta = ph.sbuf("f0ta", [128, TCH], F32)
        bcol = ph.sbuf("f0bcol2", [128, 1], F32)
        for o in range(NC):
            em.load(bcol, opb[o * 128:(o + 1) * 128].rearrange("(a b) -> a b", b=1))
            for tc in range(NT):
                for i in range(NC):
                    em.mm(ps, wop.chunk(i, o * 128, (o + 1) * 128),
                          ob[:, i * T + tc * TCH: i * T + tc * TCH + TCH],
                          start=(i == 0), stop=(i == NC - 1))
                em.act(ta, ps, AF.Identity, bias=bcol[:, 0:1])
                em.load(t2 := st, xn[o * 128:(o + 1) * 128, tc * TCH:(tc + 1) * TCH]) if False else None
                em.load(stf[:, 0:TCH], xn[o * 128:(o + 1) * 128, tc * TCH:(tc + 1) * TCH])
                em.tt(ta, ta, stf[:, 0:TCH], ALU.add)
                em.store(mha_d[o * 128:(o + 1) * 128, tc * TCH:(tc + 1) * TCH], ta)
    ln_fm(em, c, io["l0f_mha_d"], None, None, gain=ln1g, bias=ln1b, dst_f32=None) if False else None
    ln_fm(em, c, io["l0f_mha_d"], y1f_d, y1b_d, gain=ln1g, bias=ln1b, dst_lo=y1l_d)
    with Phase(em) as ph:
        t1 = ph.sbuf("g0t1", [128, 128], F32)
        pstf = ph.psum([128, 128])
        for tt_ in range(T // 128):
            for j in range(NC):
                em.load(t1, y1f_d[j * 128:(j + 1) * 128, tt_ * 128:(tt_ + 1) * 128])
                em.transpose(pstf, t1, c["ident"])
                em.copy(t1, pstf)
                em.store(y1tm_d[tt_ * 128:(tt_ + 1) * 128, j * 128:(j + 1) * 128], t1)
        y1b = ph.sbuf("g0y1b", [128, NC * T], BF16)
        for i in range(NC):
            em.load(y1b[:, i * T:(i + 1) * T], y1b_d[i * 128:(i + 1) * 128, :])
        gwt = ph.sbuf("g0gwt", [128, NC * 8], BF16)
        gwl = ph.sbuf("g0gwl", [128, NC * 8], BF16)
        gst = ph.sbuf("g0gst", [128, 8], F32)
        for i in range(NC):
            em.load(gst, wg[i * 128:(i + 1) * 128, :])
            em.act(gwt[:, i * 8:(i + 1) * 8], gst, AF.Copy)
            em.tt(gst, gst, gwt[:, i * 8:(i + 1) * 8], ALU.subtract)
            em.act(gwl[:, i * 8:(i + 1) * 8], gst, AF.Copy)
        xlo_t = ph.sbuf("g0xlo", [128, 128], BF16)
        psl = ph.psum([128, 8])
        lg = ph.sbuf("g0lg", [128, 8], F32)
        pr = ph.sbuf("g0pr", [128, 8], F32)
        v1 = ph.sbuf("g0v1", [128, 1], F32)
        m1 = ph.sbuf("g0m1", [128, 8], F32)
        evals = ph.sbuf("g0ev", [128, 8], F32)
        em.iota(evals, [[1, 8]], base=0, channel_multiplier=0)
        tidc = ph.sbuf("g0tid", [128, 1], F32)
        tidc0 = ph.sbuf("g0tid0", [128, 1], F32)
        em.iota(tidc0, [[0, 1]], base=8192, channel_multiplier=1)
        carry = ph.sbuf("g0carry", [1, 8], F32)
        em.memset(carry, 0.0, eng="vector")
        loc = ph.sbuf("g0loc", [128, 1], F32)
        slot = ph.sbuf("g0slot", [128, 1], F32)
        slotu = ph.sbuf("g0slotu", [128, 1], U32)
        ovf = ph.sbuf("g0ovf", [128, 1], F32)
        tmpc = ph.sbuf("g0tmpc", [128, 8], F32)
        psc = ph.psum([128, 8])
        pss = ph.psum([1, 8])
        trilb = ph.sbuf("g0trilb", [128, 128], F32)
        em.copy(trilb, c["triui"]) if False else None
        em.tt(trilb, c["trils"], c["ident"], ALU.add)
        ytmb = ph.sbuf("g0ytmb", [128, D], BF16)
        ytmf = ph.sbuf("g0ytmf", [128, D], F32)
        for tt_ in range(T // 128):
            for i in range(NC):
                em.load(xlo_t, y1l_d[i * 128:(i + 1) * 128, tt_ * 128:(tt_ + 1) * 128])
                em.mm(psl, y1b[:, i * T + tt_ * 128: i * T + tt_ * 128 + 128],
                      gwt[:, i * 8:(i + 1) * 8], start=(i == 0), stop=False)
                em.mm(psl, y1b[:, i * T + tt_ * 128: i * T + tt_ * 128 + 128],
                      gwl[:, i * 8:(i + 1) * 8], start=False, stop=False)
                em.mm(psl, xlo_t, gwt[:, i * 8:(i + 1) * 8],
                      start=False, stop=(i == NC - 1))
            em.copy(lg, psl)
            em.red(v1, lg, ALU.max)
            em.ts(v1, v1, -1.0, ALU.mult)
            em.act(pr, lg, AF.Exp, bias=v1[:, 0:1])
            em.red(v1, pr, ALU.add)
            em.recip(v1, v1)
            em.ts(pr, pr, v1[:, 0:1], ALU.mult)
            em.store(gates_d[tt_ * 128:(tt_ + 1) * 128, :], pr)
            em.red(v1, lg, ALU.max)
            em.ts(m1, lg, v1[:, 0:1], ALU.is_ge)
            em.mm(psc, trilb, m1, start=True, stop=False)
            em.mm(psc, c["ones"][0:1, 0:128], carry, start=False, stop=True)
            em.tt(tmpc, psc, m1, ALU.mult)
            em.red(loc, tmpc, ALU.add)
            em.ts(loc, loc, -1.0, ALU.add)
            em.mm(pss, c["ones"][0:128, 0:1], m1)
            em.tt(carry, carry, pss, ALU.add)
            em.tt(tmpc, m1, evals, ALU.mult)
            em.red(slot, tmpc, ALU.add)
            em.ts(slot, slot, 1024.0, ALU.mult)
            em.tt(slot, slot, loc, ALU.add)
            em.ts(ovf, loc, 1023.5, ALU.is_gt)
            em.ts(tidc, tidc0, float(tt_ * 128), ALU.add, eng="vector")
            em.tt(tidc, tidc, slot, ALU.subtract)
            em.tt(tidc, tidc, ovf, ALU.mult)
            em.tt(slot, slot, tidc, ALU.add)
            em.copy(slotu, slot)
            em.store(io["l0f_slot_d"][tt_ * 128:(tt_ + 1) * 128, :], slot,
                     allow_slow_non_contiguous=True)
            if not io.get("l0f_fused"):
                em.load(ytmf, y1tm_d[tt_ * 128:(tt_ + 1) * 128, :])
                em.scatter(xdisp_d, ytmf, slotu[:, 0:1], axis=0, bounds=8192 + T - 1)
            em.ts(slot, slot, 8191.0, ALU.min)
            em.copy(slotu, slot)
            em.store(io["l0f_gbidx_d"][tt_ * 128:(tt_ + 1) * 128, :], slotu)
    if part == "all":
        _l0_expert_part(em, c, io)


def _l0_expert_part(em, c, io):
    w1bf = io["l0f_w1bf"]; w2bf = io["l0f_w2bf"]
    b1 = io["l0f_b1"]; b2 = io["l0f_b2"]
    xdisp_d = io["l0f_xdisp_d"]; yall_d = io["l0f_yall_d"]
    ffnraw_d = io["l0f_ffnraw_d"]
    with Phase(em) as ph:
        cast_dram_bf(em, ph, io["l0f_w1"].rearrange("e a b -> (e a) b"), w1bf, 8192, 4096, "c0w1")
        cast_dram_bf(em, ph, io["l0f_w2"].rearrange("e a b -> (e a) b"), w2bf, 8 * 4096, 1024, "c0w2")
    with Phase(em) as ph:
        xe = ph.sbuf("e0xe", [128, NC * 1024], BF16)
        hsb = ph.sbuf("e0hsb", [128, 32 * 1024], BF16)
        pool = TilePool(ph, "e0w", 8, [128, 1024], BF16)
        ps = ph.psum([128, 512])
        ps2_ = ph.psum([128, 512])
        psb = ph.psum([128, 512])
        psb2 = ph.psum([128, 512])
        bcol = ph.sbuf("e0bcol", [128, 1], F32)
        brow = ph.sbuf("e0brow", [1, 1024], F32)
        ta = ph.sbuf("e0ta", [128, 1024], F32)
        tb = ph.sbuf("e0tb", [128, 1024], BF16)
        for e in range(8):
            for i in range(NC):
                if io.get("l0f_fused"):
                    em.load(xe[:, i * 1024:(i + 1) * 1024],
                            io["l0f_xdispfm_d"][i * 128:(i + 1) * 128, e * 1024:(e + 1) * 1024])
                else:
                    em.load(xe[:, i * 1024:(i + 1) * 1024],
                            xdisp_d[e * 1024:(e + 1) * 1024, i * 128:(i + 1) * 128],
                            transpose=True)
            for f in range(32):
                em.load(bcol, b1[e, f * 128:(f + 1) * 128].rearrange("(a b) -> a b", b=1))
                for i in range(NC):
                    wt = pool.next()
                    em.load(wt[:, 0:128], w1bf[e * 1024 + i * 128: e * 1024 + (i + 1) * 128,
                                               f * 128:(f + 1) * 128])
                    em.mm(ps, wt[:, 0:128], xe[:, i * 1024: i * 1024 + 512],
                          start=(i == 0), stop=(i == NC - 1))
                    em.mm(ps2_, wt[:, 0:128], xe[:, i * 1024 + 512:(i + 1) * 1024],
                          start=(i == 0), stop=(i == NC - 1))
                em.act(ta[:, 0:512], ps, AF.Identity, bias=bcol[:, 0:1])
                em.act(ta[:, 512:1024], ps2_, AF.Identity, bias=bcol[:, 0:1])
                em.act(hsb[:, f * 1024:(f + 1) * 1024], ta, AF.Copy)
            em.load(brow, b2[e].rearrange("(a b) -> a b", a=1))
            em.mm(psb, c["ones"][0:1, 0:128], brow[0:1, 0:512])
            em.mm(psb2, c["ones"][0:1, 0:128], brow[0:1, 512:1024])
            em.copy(ta[:, 0:512], psb)
            em.copy(ta[:, 512:1024], psb2)
            for st_ in range(8):
                for f in range(32):
                    wt = pool.next()
                    em.load(wt, w2bf[e * 4096 + f * 128: e * 4096 + (f + 1) * 128, :])
                    em.mm(ps, hsb[:, f * 1024 + st_ * 128: f * 1024 + st_ * 128 + 128],
                          wt[:, 0:512], start=(f == 0), stop=(f == 31))
                    em.mm(ps2_, hsb[:, f * 1024 + st_ * 128: f * 1024 + st_ * 128 + 128],
                          wt[:, 512:1024], start=(f == 0), stop=(f == 31))
                em.tt(tb[:, 0:512], ps, ta[:, 0:512], ALU.add)
                em.tt(tb[:, 512:1024], ps2_, ta[:, 512:1024], ALU.add)
                em.store(yall_d[e * 1024 + st_ * 128: e * 1024 + (st_ + 1) * 128, :], tb)
    with Phase(em) as ph:
        gbi = ph.sbuf("gb0i", [128, 1], U32)
        row = ph.sbuf("gb0row", [128, D], BF16)
        for tt_ in range(T // 128):
            em.load(gbi, io["l0f_gbidx_d"][tt_ * 128:(tt_ + 1) * 128, :])
            em.gather(row, yall_d, gbi[:, 0:1], axis=0, bounds=8191)
            em.store(ffnraw_d[tt_ * 128:(tt_ + 1) * 128, :], row)


def build_l0_dispatch_mm(em, c, io):
    y1tm_d = io["l0f_y1tm_d"]
    slot_d = io["l0f_slot_d"]
    xfm = io["l0f_xdispfm_d"]
    with Phase(em) as ph:
        ytb = ph.sbuf("dm0ytb", [128, 32 * 1024], BF16)
        for t in range(32):
            em.load(ytb[:, t * 1024:(t + 1) * 1024], y1tm_d[t * 128:(t + 1) * 128, :],
                    eng="gpsimd")
        slots = ph.sbuf("dm0slots", [128, 32], F32)
        em.load(slots, slot_d.rearrange("(a b) c -> b (a c)", b=128),
                allow_slow_non_contiguous=True)
        srow = ph.sbuf("dm0srow", [128, TCH], F32)
        pall = ph.sbuf("dm0pall", [128, 32 * TCH], BF16)
        ps = ph.psum([128, TCH])
        xev = ph.sbuf("dm0xev", [128, TCH], BF16)
        for sc in range(16):
            em.iota(srow, [[1, TCH]], base=sc * TCH, channel_multiplier=0)
            for t in range(32):
                em.ts(pall[:, t * TCH:(t + 1) * TCH], srow, slots[:, t:t + 1], ALU.is_equal)
            for cc in range(NC):
                for t in range(32):
                    em.mm(ps, ytb[:, t * 1024 + cc * 128: t * 1024 + (cc + 1) * 128],
                          pall[:, t * TCH:(t + 1) * TCH], start=(t == 0), stop=(t == 31))
                em.act(xev, ps, AF.Copy)
                em.store(xfm[cc * 128:(cc + 1) * 128, sc * TCH:(sc + 1) * TCH], xev)


def build_l0_ffn_B(em, c, io):
    gates_all = io["l0b_gates_all"]
    ownpos = io.get("l0b_ownpos")
    y1tm = io["l0b_y1tm"]
    ffnraw = io["l0b_ffnraw"]
    ln2g, ln2b = io["l0b_ln2_g"], io["l0b_ln2_b"]
    kg_all = io["l0b_kg_all"]
    out_tm = io["l0b_out_tm"]
    GT_ = 8 * T
    with Phase(em) as ph:
        g = ph.sbuf("b0g", [128, 8], F32)
        v1 = ph.sbuf("b0v1", [128, 1], F32)
        m1 = ph.sbuf("b0m1", [128, 8], F32)
        tmpc = ph.sbuf("b0tmp", [128, 8], F32)
        loc = ph.sbuf("b0loc", [128, 1], F32)
        keep = ph.sbuf("b0keep", [128, 1], F32)
        carry = ph.sbuf("b0carry", [1, 8], F32)
        em.memset(carry, 0.0, eng="vector")
        trilb = ph.sbuf("b0trilb", [128, 128], F32)
        em.tt(trilb, c["trils"], c["ident"], ALU.add)
        psc = ph.psum([128, 8])
        pss = ph.psum([1, 8])
        fused = bool(io.get("l0b_fused_order"))
        def _maprow(g_):
            if not fused:
                return g_ * 128
            l = g_ // 32
            blk = g_ % 32
            cid_own = blk // 4
            return cid_own * T + l * NS + (blk % 4) * 128
        for tt_ in range(GT_ // 128):
            row0 = _maprow(tt_)
            em.load(g, gates_all[row0:row0 + 128, :])
            em.red(v1, g, ALU.max)
            em.ts(m1, g, v1[:, 0:1], ALU.is_ge)
            em.mm(psc, trilb, m1, start=True, stop=False)
            em.mm(psc, c["ones"][0:1, 0:128], carry, start=False, stop=True)
            em.tt(tmpc, psc, m1, ALU.mult)
            em.red(loc, tmpc, ALU.add)
            em.ts(loc, loc, -1.0, ALU.add)
            em.mm(pss, c["ones"][0:128, 0:1], m1)
            em.tt(carry, carry, pss, ALU.add)
            em.ts(keep, loc, 4095.5, ALU.is_gt)
            em.ts(keep, keep, -1.0, ALU.mult, 1.0, ALU.add)
            em.tt(keep, keep, v1, ALU.mult)
            em.store(kg_all[row0:row0 + 128, 0:1], keep, allow_slow_non_contiguous=True)
    if io.get("l0b_fused_order"):
        kg_own = io["l0b_kg_own"]
        tok_kga = io["l0b_kg_all"].last_w
        em.raw_wait("scalar", tok_kga)
        def _kg_slice(E, sv, mk):
            mk(E.dma_start(out=kg_own.h[:, :],
                           in_=io["l0b_kg_all"].h[bass.ts(sv, T), :]))
        kg_own.last_w = em.dyn_region("scalar", _kg_slice, 1)
    with Phase(em) as ph:
        pos = ph.sbuf("b1pos", [128, 1], U32)
        kg = ph.sbuf("b1kg", [128, 8], F32)
        y1t = ph.sbuf("b1y1", [128, D], F32)
        fr = ph.sbuf("b1fr", [128, D], F32)
        grow = ph.sbuf("b1grow", [1, D], F32)
        brow = ph.sbuf("b1brow", [1, D], F32)
        gb = ph.psum([128, 512])
        gbs = ph.sbuf("b1gbs", [128, D], F32)
        bbs = ph.sbuf("b1bbs", [128, D], F32)
        em.load(grow, ln2g.rearrange("(a b) -> a b", a=1))
        em.load(brow, ln2b.rearrange("(a b) -> a b", a=1))
        for half in range(2):
            em.mm(gb, c["ones"][0:1, 0:128], grow[0:1, half * 512:(half + 1) * 512])
            em.copy(gbs[:, half * 512:(half + 1) * 512], gb)
            em.mm(gb, c["ones"][0:1, 0:128], brow[0:1, half * 512:(half + 1) * 512])
            em.copy(bbs[:, half * 512:(half + 1) * 512], gb)
        mrow = ph.sbuf("b1mrow", [128, 1], F32)
        vrow = ph.sbuf("b1vrow", [128, 1], F32)
        for tt_ in range(T // 128):
            if io.get("l0b_fused_order"):
                em.load(kg, io["l0b_kg_own"][tt_ * 128:(tt_ + 1) * 128, :])
            else:
                em.load(pos, ownpos[tt_ * 128:(tt_ + 1) * 128, :])
                em.gather(kg, kg_all, pos[:, 0:1], axis=0, bounds=GT_ - 1)
            em.load(y1t, y1tm[tt_ * 128:(tt_ + 1) * 128, :])
            em.load(fr, ffnraw[tt_ * 128:(tt_ + 1) * 128, :])
            em.ts(fr, fr, kg[:, 0:1], ALU.mult)
            em.tt(y1t, y1t, fr, ALU.add)
            em.red(mrow, y1t, ALU.add)
            em.ts(mrow, mrow, 1.0 / D, ALU.mult)
            em.ts(fr, y1t, mrow[:, 0:1], ALU.subtract)
            em.tt(y1t, fr, fr, ALU.mult)
            em.red(vrow, y1t, ALU.add)
            em.ts(vrow, vrow, 1.0 / D, ALU.mult, 1e-5, ALU.add)
            em.act(vrow, vrow, AF.Sqrt) if False else None
            em.act(mrow, vrow, AF.Sqrt)
            em.recip(vrow, mrow)
            em.ts(fr, fr, vrow[:, 0:1], ALU.mult)
            em.tt(fr, fr, gbs, ALU.mult)
            em.tt(fr, fr, bbs, ALU.add)
            em.store(out_tm[tt_ * 128:(tt_ + 1) * 128, :], fr)



def _fm(x):
    return np.ascontiguousarray(np.asarray(x).T).astype(np.float32)


def _wprep(inputs):
    d = inputs
    W = {}
    tr = lambda a: np.ascontiguousarray(np.asarray(a).T).astype(np.float32)
    eT = lambda w: np.ascontiguousarray(np.swapaxes(np.asarray(w), -1, -2)).astype(np.float32)
    f32 = lambda a: np.ascontiguousarray(np.asarray(a)).astype(np.float32)
    W["l0a_w_in"] = f32(d["L0_attn_w_in"]); W["l0a_b_in"] = f32(d["L0_attn_b_in"])
    W["l0a_w_a"] = f32(d["L0_attn_w_a"]); W["l0a_b_a"] = f32(d["L0_attn_b_a"])
    W["l0a_a_param"] = f32(d["L0_attn_a_param"])
    W["l0f_in_proj_wT"] = tr(d["L0_ffn_in_proj_w"]); W["l0f_in_proj_b"] = f32(d["L0_ffn_in_proj_b"])
    W["l0f_out_proj_wT"] = tr(d["L0_ffn_out_proj_w"]); W["l0f_out_proj_b"] = f32(d["L0_ffn_out_proj_b"])
    W["l0f_ln1_g"] = f32(d["L0_ffn_ln1_g"]); W["l0f_ln1_b"] = f32(d["L0_ffn_ln1_b"])
    W["l0b_ln2_g"] = f32(d["L0_ffn_ln2_g"]); W["l0b_ln2_b"] = f32(d["L0_ffn_ln2_b"])
    W["l0f_wg"] = f32(d["L0_ffn_wg"]); W["l0f_w1"] = f32(d["L0_ffn_w1"])
    W["l0f_b1"] = f32(d["L0_ffn_b1"]); W["l0f_w2"] = f32(d["L0_ffn_w2"]); W["l0f_b2"] = f32(d["L0_ffn_b2"])
    W["l1a_Wq"] = f32(d["L1_attn_Wq"]); W["l1a_Wk"] = f32(d["L1_attn_Wk"])
    W["l1a_Wv"] = f32(d["L1_attn_Wv"]); W["l1a_Wb"] = f32(d["L1_attn_Wb"])
    W["l1a_conv_q"] = f32(d["L1_attn_conv_q_w"]); W["l1a_conv_k"] = f32(d["L1_attn_conv_k_w"])
    W["l1a_conv_v"] = f32(d["L1_attn_conv_v_w"])
    W["l1a_fir_short"] = f32(d["L1_attn_fir_short_w"]).reshape(1024, 7)
    W["l1a_fir_long"] = f32(d["L1_attn_fir_long_w"]).reshape(1024, 64)
    W["l1a_Wf1"] = f32(d["L1_attn_Wf1"]); W["l1a_bf1"] = f32(d["L1_attn_bf1"])
    W["l1a_Wf2"] = f32(d["L1_attn_Wf2"]); W["l1a_bf2"] = f32(d["L1_attn_bf2"])
    W["l1a_rms_w"] = np.tile(f32(d["L1_attn_rms_w"]), 4)
    W["l1a_Wo"] = f32(d["L1_attn_Wo"])
    W["l1f_WrT"] = tr(d["L1_ffn_W_r"]); W["l1f_WkT"] = tr(d["L1_ffn_W_k"])
    W["l1f_WvT"] = tr(d["L1_ffn_W_v"])
    W["l1f_mu_r"] = f32(d["L1_ffn_mu_r"]); W["l1f_mu_k"] = f32(d["L1_ffn_mu_k"])
    W["l2a_Wq"] = f32(d["L2_attn_Wq"]); W["l2a_Wk"] = f32(d["L2_attn_Wk"])
    W["l2a_Wv"] = f32(d["L2_attn_Wv"]); W["l2a_Wo"] = f32(d["L2_attn_Wo"])
    W["l2f_w_shared_in"] = f32(d["L2_ffn_w_shared_in"]); W["l2f_w_shared_out"] = f32(d["L2_ffn_w_shared_out"])
    W["l2f_w1_shared"] = f32(d["L2_ffn_w1_shared"]); W["l2f_w2"] = f32(d["L2_ffn_w2"])
    W["l2f_w_gate"] = f32(d["L2_ffn_w_gate"]); W["l2f_gate_bias"] = f32(d["L2_ffn_gate_bias"])
    W["l3a_Wg"] = f32(d["L3_attn_Wg"]); W["l3a_bg"] = f32(d["L3_attn_bg"])
    W["l3a_Wv"] = f32(d["L3_attn_Wv"]); W["l3a_bv"] = f32(d["L3_attn_bv"])
    W["l3a_Wd"] = f32(d["L3_attn_Wd"]); W["l3a_bd"] = f32(d["L3_attn_bd"])
    W["l3a_gate_W"] = f32(d["L3_attn_gate_W"])
    W["l3f_g1"] = f32(d["L3_ffn_g1"]); W["l3f_g2"] = f32(d["L3_ffn_g2"])
    W["l3f_WqT"] = tr(d["L3_ffn_Wq"]); W["l3f_WkcT"] = tr(d["L3_ffn_Wkc"])
    W["l3f_WvcT"] = tr(d["L3_ffn_Wvc"]); W["l3f_WoT"] = tr(d["L3_ffn_Wo"])
    W["l3f_WrT"] = tr(d["L3_ffn_Wr"]); W["l3f_expert_bias"] = f32(d["L3_ffn_expert_bias"])
    W["l3f_sW1T"] = tr(d["L3_ffn_sW1"]); W["l3f_sW2T"] = tr(d["L3_ffn_sW2"])
    W["l3f_sW3T"] = tr(d["L3_ffn_sW3"])
    W["l3f_eW1T"] = eT(d["L3_ffn_eW1"]); W["l3f_eW2T"] = eT(d["L3_ffn_eW2"])
    W["l3f_eW3T"] = eT(d["L3_ffn_eW3"])
    return W


def _ext_in(em, io, W, names):
    for nm in names:
        io[nm] = em.dram(nm, list(W[nm].shape), F32, kind="ExternalInput")


_L1A_W = ["l1a_Wq", "l1a_Wk", "l1a_Wv", "l1a_Wb", "l1a_conv_q", "l1a_conv_k", "l1a_conv_v",
          "l1a_fir_short", "l1a_fir_long", "l1a_Wf1", "l1a_bf1", "l1a_Wf2", "l1a_bf2",
          "l1a_rms_w", "l1a_Wo"]
_L1F_W = ["l1f_WrT", "l1f_WkT", "l1f_WvT", "l1f_mu_r", "l1f_mu_k"]
_L2A_W = ["l2a_Wq", "l2a_Wk", "l2a_Wv", "l2a_Wo"]
_L2F_W = ["l2f_w_shared_in", "l2f_w_shared_out", "l2f_w1_shared", "l2f_w2",
          "l2f_w_gate", "l2f_gate_bias"]
_L3A_W = ["l3a_Wg", "l3a_bg", "l3a_Wv", "l3a_bv", "l3a_Wd", "l3a_bd", "l3a_gate_W"]
_L3F_W = ["l3f_g1", "l3f_g2", "l3f_WqT", "l3f_WkcT", "l3f_WvcT", "l3f_WoT", "l3f_WrT",
          "l3f_expert_bias", "l3f_sW1T", "l3f_sW2T", "l3f_sW3T", "l3f_eW1T", "l3f_eW2T",
          "l3f_eW3T"]
_L0A_W = ["l0a_w_in", "l0a_b_in", "l0a_w_a", "l0a_b_a", "l0a_a_param"]
_L0F_FRONT_W = ["l0f_in_proj_wT", "l0f_in_proj_b", "l0f_out_proj_wT", "l0f_out_proj_b",
                "l0f_ln1_g", "l0f_ln1_b", "l0f_wg", "l0f_w1", "l0f_b1", "l0f_w2", "l0f_b2"]


def _scratch_l1a(em, io):
    for nm, shp, dt in (("l1a_q_d", [D, T], F32), ("l1a_k_d", [D, T], F32),
                        ("l1a_v_d", [D, T], F32), ("l1a_qn_d", [D, T], BF16),
                        ("l1a_kn_d", [D, T], BF16), ("l1a_beta_d", [4, T], F32),
                        ("l1a_fw_d", [16, T], F32), ("l1a_delta_d", [D, T], F32),
                        ("l1a_comb_d", [D, T], F32)):
        io[nm] = em.dram(nm, shp, dt)


def _scratch_l2(em, io):
    for nm, shp, dt in (("l2a_q_scratch", [D, T], BF16), ("l2a_k_scratch", [D, T], BF16),
                        ("l2a_ktm_scratch", [T, D], BF16), ("l2a_vtm_scratch", [T, D], BF16),
                        ("l2a_o_scratch", [D, T], BF16), ("l2f_wd_scratch", [8, T], F32),
                        ("l2f_wdbf_scratch", [8, T], BF16),
                        ("l2f_wbf_scratch", [13312, 1024], BF16)):
        io[nm] = em.dram(nm, shp, dt)


def _scratch_l3(em, io):
    for nm, shp, dt in (("l3a_comb_scratch", [4, T], F32), ("l3a_a_scratch", [D, T], F32),
                        ("l3f_nx_f", [D, T], F32), ("l3f_nx_b", [D, T], BF16),
                        ("l3f_nx_l", [D, T], BF16), ("l3f_x2_f", [D, T], F32),
                        ("l3f_q_scratch", [D, T], BF16), ("l3f_k_scratch", [256, T], BF16),
                        ("l3f_vtm_scratch", [T, 256], BF16), ("l3f_crows", [8, T], BF16),
                        ("l3f_crows_f32", [8, T], F32), ("l3f_sh_scratch", [D, T], F32),
                        ("l3f_wbf_scratch", [24 * 1024, 1024], BF16)):
        io[nm] = em.dram(nm, shp, dt)


_prog_cache = {}


def _run(nc, in_maps):
    return run_bass_kernel_spmd(nc, in_maps, core_ids=list(range(NCORES)))


def kernel_unfused(**inputs):
    W = _wprep(inputs)
    x0 = np.asarray(inputs["L0_attn_x"], np.float32)
    x_fm = [np.ascontiguousarray(x0[cid].T) for cid in range(NCORES)]

    nc = bass.Bass("TRN2", target_bir_lowering=False)
    em = Emit(nc)
    io = {"x_in": em.dram("x_in", [D, T], F32, kind="ExternalInput"),
          "xn_fm": em.dram("xn_fm", [D, T], F32, kind="ExternalOutput"),
          "xn_bf": em.dram("xn_bf", [D, T], BF16, kind="ExternalOutput"),
          "out_fm": em.dram("out_fm", [D, T], F32, kind="ExternalOutput")}
    _ext_in(em, io, W, _L0A_W)
    c = make_consts(em)
    ln_fm(em, c, io["x_in"], io["xn_fm"], io["xn_bf"])
    build_l0_attn(em, c, io)
    em.finalize()
    res1 = _run(nc, [dict({"x_in": x_fm[cid]}, **{k: W[k] for k in _L0A_W})
                     for cid in range(NCORES)])
    attn0 = [np.asarray(res1.results[cid]["out_fm"]) for cid in range(NCORES)]
    xn0 = [np.asarray(res1.results[cid]["xn_fm"]) for cid in range(NCORES)]
    xn0b = [np.asarray(res1.results[cid]["xn_bf"]) for cid in range(NCORES)]

    def nshard(mats, dtype):
        out = []
        for cid in range(NCORES):
            cols = [mats[l][:, cid * NS:(cid + 1) * NS] for l in range(8)]
            out.append(np.ascontiguousarray(np.concatenate(cols, 1), dtype=dtype))
        return out
    xn0_ns = nshard(xn0, np.float32)
    xn0b_ns = nshard(xn0b, xn0b[0].dtype)

    nc = bass.Bass("TRN2", target_bir_lowering=False)
    em = Emit(nc)
    io = {"l0f_xn_fm": em.dram("l0f_xn_fm", [D, T], F32, kind="ExternalInput"),
          "l0f_xn_bf": em.dram("l0f_xn_bf", [D, T], BF16, kind="ExternalInput")}
    _ext_in(em, io, W, _L0F_FRONT_W)
    for nm, shp, dt, kind in (("l0f_qkv_d", [3072, T], BF16, "Internal"),
                              ("l0f_qtm_d", [T, 1024], BF16, "Internal"),
                              ("l0f_ktm_d", [T, 1024], BF16, "Internal"),
                              ("l0f_vtm_d", [T, 1024], BF16, "Internal"),
                              ("l0f_mha_d", [D, T], F32, "Internal"),
                              ("l0f_y1f_d", [D, T], F32, "Internal"),
                              ("l0f_y1b_d", [D, T], BF16, "Internal"),
                              ("l0f_y1l_d", [D, T], BF16, "Internal"),
                              ("l0f_y1tm_d", [T, D], F32, "ExternalOutput"),
                              ("l0f_gates_d", [T, 8], F32, "ExternalOutput"),
                              ("l0f_xdisp_d", [8192 + T, D], BF16, "ExternalOutput"),
                              ("l0f_gbidx_d", [T, 1], U32, "ExternalOutput"),
                              ("l0f_yall_d", [8192, D], BF16, "Internal"),
                              ("l0f_ffnraw_d", [T, D], F32, "Internal"),
                              ("l0f_w1bf", [8, 4096], BF16, "Internal"),
                              ("l0f_w2bf", [8, 1024], BF16, "Internal"),
                              ("l0f_slot_d", [T, 1], F32, "Internal")):
        io[nm] = em.dram(nm, shp, dt, kind=kind)
    c = make_consts(em)
    build_l0_ffn_A(em, c, io, part="front")
    em.finalize()
    res2 = _run(nc, [dict({"l0f_xn_fm": xn0_ns[cid], "l0f_xn_bf": xn0b_ns[cid]},
                          **{k: W[k] for k in _L0F_FRONT_W}) for cid in range(NCORES)])

    nc = bass.Bass("TRN2", target_bir_lowering=False)
    em = Emit(nc)
    io = {}
    for nm in ("l0f_w1", "l0f_w2", "l0f_b1", "l0f_b2"):
        io[nm] = em.dram(nm, list(W[nm].shape), F32, kind="ExternalInput")
    io["l0f_xdisp_d"] = em.dram("l0f_xdisp_d", [8192 + T, D], BF16, kind="ExternalInput")
    io["l0f_gbidx_d"] = em.dram("l0f_gbidx_d", [T, 1], U32, kind="ExternalInput")
    io["l0f_w1bf"] = em.dram("l0f_w1bf", [8192, 4096], BF16)
    io["l0f_w2bf"] = em.dram("l0f_w2bf", [8 * 4096, 1024], BF16)
    io["l0f_yall_d"] = em.dram("l0f_yall_d", [8192, D], BF16)
    io["l0f_ffnraw_d"] = em.dram("l0f_ffnraw_d", [T, D], F32, kind="ExternalOutput")
    c = make_consts(em)
    build_l0_ffn_A(em, c, io, part="back")
    em.finalize()
    res3 = _run(nc, [{"l0f_w1": W["l0f_w1"], "l0f_w2": W["l0f_w2"],
                      "l0f_b1": W["l0f_b1"], "l0f_b2": W["l0f_b2"],
                      "l0f_xdisp_d": np.asarray(res2.results[cid]["l0f_xdisp_d"]),
                      "l0f_gbidx_d": np.asarray(res2.results[cid]["l0f_gbidx_d"])}
                     for cid in range(NCORES)])

    gates_global = np.zeros((8 * T, 8), np.float32)
    ownpos = []
    l_idx = np.repeat(np.arange(8), NS)
    j_idx = np.tile(np.arange(NS), 8)
    for cid in range(NCORES):
        pos = (l_idx * T + cid * NS + j_idx).astype(np.uint32)
        ownpos.append(np.ascontiguousarray(pos[:, None]))
        gates_global[pos] = np.asarray(res2.results[cid]["l0f_gates_d"])
    nc = bass.Bass("TRN2", target_bir_lowering=False)
    em = Emit(nc)
    io = {"l0b_gates_all": em.dram("l0b_gates_all", [8 * T, 8], F32, kind="ExternalInput"),
          "l0b_ownpos": em.dram("l0b_ownpos", [T, 1], U32, kind="ExternalInput"),
          "l0b_y1tm": em.dram("l0b_y1tm", [T, D], F32, kind="ExternalInput"),
          "l0b_ffnraw": em.dram("l0b_ffnraw", [T, D], F32, kind="ExternalInput"),
          "l0b_ln2_g": em.dram("l0b_ln2_g", [D], F32, kind="ExternalInput"),
          "l0b_ln2_b": em.dram("l0b_ln2_b", [D], F32, kind="ExternalInput"),
          "l0b_kg_all": em.dram("l0b_kg_all", [8 * T, 8], F32),
          "l0b_out_tm": em.dram("l0b_out_tm", [T, D], F32, kind="ExternalOutput")}
    c = make_consts(em)
    build_l0_ffn_B(em, c, io)
    em.finalize()
    res4 = _run(nc, [{"l0b_gates_all": gates_global, "l0b_ownpos": ownpos[cid],
                      "l0b_y1tm": np.asarray(res2.results[cid]["l0f_y1tm_d"]),
                      "l0b_ffnraw": np.asarray(res3.results[cid]["l0f_ffnraw_d"]),
                      "l0b_ln2_g": W["l0b_ln2_g"], "l0b_ln2_b": W["l0b_ln2_b"]}
                     for cid in range(NCORES)])
    out_ns = [np.asarray(res4.results[cid]["l0b_out_tm"]).reshape(8, NS, D) for cid in range(NCORES)]
    ffn0 = []
    for l in range(8):
        cols = np.concatenate([out_ns[cid][l] for cid in range(NCORES)], 0)
        ffn0.append(np.ascontiguousarray(cols.T))

    prev_x, prev_attn, prev_ffn = x_fm, attn0, ffn0
    for layer in (1, 2, 3):
        nc = bass.Bass("TRN2", target_bir_lowering=False)
        em = Emit(nc)
        io = {"x_prev": em.dram("x_prev", [D, T], F32, kind="ExternalInput"),
              "attn_prev": em.dram("attn_prev", [D, T], F32, kind="ExternalInput"),
              "ffn_prev": em.dram("ffn_prev", [D, T], F32, kind="ExternalInput"),
              "x_fm": em.dram("x_fm", [D, T], F32, kind="ExternalOutput"),
              "xn_fm": em.dram("xn_fm", [D, T], F32),
              "xn_bf": em.dram("xn_bf", [D, T], BF16),
              "xn_lo": em.dram("xn_lo", [D, T], BF16)}
        wnames = {1: _L1A_W + _L1F_W, 2: _L2A_W + _L2F_W, 3: _L3A_W + _L3F_W}[layer]
        _ext_in(em, io, W, wnames)
        c = make_consts(em)
        add3_fm(em, io["x_fm"], io["x_prev"], io["attn_prev"], io["ffn_prev"])
        ln_fm(em, c, io["x_fm"], io["xn_fm"], io["xn_bf"], dst_lo=io["xn_lo"])
        io["attn_out"] = em.dram("attn_out", [D, T], F32, kind="ExternalOutput")
        io["ffn_out"] = em.dram("ffn_out", [D, T], F32, kind="ExternalOutput")
        if layer == 1:
            _scratch_l1a(em, io)
            io["out_fm"] = io["attn_out"]
            build_l1_attn(em, c, io)
            io["out_fm"] = io["ffn_out"]
            build_l1_ffn(em, c, io)
        elif layer == 2:
            _scratch_l2(em, io)
            io["out_fm"] = io["attn_out"]
            build_l2_attn(em, c, io)
            io["out_fm"] = io["ffn_out"]
            build_l2_ffn(em, c, io)
        else:
            _scratch_l3(em, io)
            io["out_fm"] = io["attn_out"]
            build_l3_attn(em, c, io)
            io["out_fm"] = io["ffn_out"]
            build_l3_ffn(em, c, io)
        em.finalize()
        res = _run(nc, [dict({"x_prev": prev_x[cid], "attn_prev": prev_attn[cid],
                              "ffn_prev": prev_ffn[cid]},
                             **{k: W[k] for k in wnames}) for cid in range(NCORES)])
        prev_x = [np.asarray(res.results[cid]["x_fm"]) for cid in range(NCORES)]
        prev_attn = [np.asarray(res.results[cid]["attn_out"]) for cid in range(NCORES)]
        prev_ffn = [np.asarray(res.results[cid]["ffn_out"]) for cid in range(NCORES)]

    nc = bass.Bass("TRN2", target_bir_lowering=False)
    em = Emit(nc)
    io = {"x_prev": em.dram("x_prev", [D, T], F32, kind="ExternalInput"),
          "attn_prev": em.dram("attn_prev", [D, T], F32, kind="ExternalInput"),
          "ffn_prev": em.dram("ffn_prev", [D, T], F32, kind="ExternalInput"),
          "x_fm": em.dram("x_fm", [D, T], F32, kind="ExternalOutput")}
    c = make_consts(em)
    add3_fm(em, io["x_fm"], io["x_prev"], io["attn_prev"], io["ffn_prev"])
    em.finalize()
    res = _run(nc, [{"x_prev": prev_x[cid], "attn_prev": prev_attn[cid],
                     "ffn_prev": prev_ffn[cid]} for cid in range(NCORES)])
    out = np.stack([np.asarray(res.results[cid]["x_fm"]).T for cid in range(NCORES)], 0)
    return np.ascontiguousarray(out).astype(np.float32)



_dbg_em = [None]

def build_fused(W_shapes):
    nc = bass.Bass("TRN2", target_bir_lowering=False)
    em = Emit(nc)
    _dbg_em[0] = em
    io = {}
    io["x_in"] = em.dram("x_in", [D, T], F32, kind="ExternalInput")
    for nm, shp in W_shapes.items():
        io[nm] = em.dram(nm, list(shp), F32, kind="ExternalInput")
    io["x_out"] = em.dram("x_out", [D, T], F32, kind="ExternalOutput")
    c = make_consts(em)
    def sc(name, shape, dt):
        io[name] = em.dram(name, list(shape), dt)
        return io[name]
    xn_fm = sc("xn_fm", [D, T], F32)
    xn_bf = sc("xn_bf", [D, T], BF16)
    xn_lo = sc("xn_lo", [D, T], BF16)
    attn_o = sc("attn_o", [D, T], F32)
    ffn_o = sc("ffn_o", [D, T], F32)
    x_cur = sc("x_cur", [D, T], F32)

    ln_fm(em, c, io["x_in"], xn_fm, xn_bf)
    io["out_fm"] = attn_o
    build_l0_attn(em, c, io)
    io["x_ns"] = em.dram("x_ns", [D, T], F32, kind="ExternalInput")
    xn_ns = sc("xn_ns", [D, T], F32)
    xn_ns_bf = sc("xn_ns_bf", [D, T], BF16)
    ln_fm(em, c, io["x_ns"], xn_ns, xn_ns_bf)
    io2 = dict(io)
    io2["l0f_fused"] = True
    io2["l0f_xn_fm"] = xn_ns
    io2["l0f_xn_bf"] = xn_ns_bf
    for nm, shp, dt in (("l0f_qkv_d", [3072, T], BF16), ("l0f_qtm_d", [T, 1024], BF16),
                        ("l0f_ktm_d", [T, 1024], BF16), ("l0f_vtm_d", [T, 1024], BF16),
                        ("l0f_mha_d", [D, T], F32), ("l0f_y1f_d", [D, T], F32),
                        ("l0f_y1b_d", [D, T], BF16), ("l0f_y1l_d", [D, T], BF16),
                        ("l0f_y1tm_d", [T, D], F32), ("l0f_gates_d", [T, 8], F32),
                        ("l0f_xdisp_d", [1, 1], BF16), ("l0f_gbidx_d", [T, 1], U32),
                        ("l0f_yall_d", [8192, D], BF16), ("l0f_ffnraw_d", [T, D], F32),
                        ("l0f_w1bf", [8192, 4096], BF16), ("l0f_w2bf", [8 * 4096, 1024], BF16),
                        ("l0f_slot_d", [T, 1], F32), ("l0f_xdispfm_d", [D, 8192], BF16)):
        io2[nm] = sc(nm, shp, dt)
    build_l0_ffn_A(em, c, io2, part="front")
    build_l0_dispatch_mm(em, c, io2)
    _l0_expert_part(em, c, io2)
    ag_gates = sc("ag_gates", [NCORES * T, 8], F32)
    em.collective("AllGather", ALU.bypass, [io2["l0f_gates_d"]], [ag_gates])
    io3 = dict(io2)
    io3["l0b_gates_all"] = ag_gates
    io3["l0b_ffnraw"] = io2["l0f_ffnraw_d"]
    io3["l0b_y1tm"] = io2["l0f_y1tm_d"]
    io3["l0b_ln2_g"] = io["l0b_ln2_g"]
    io3["l0b_ln2_b"] = io["l0b_ln2_b"]
    io3["l0b_kg_all"] = sc("l0b_kg_all", [NCORES * T, 8], F32)
    io3["l0b_out_tm"] = sc("l0b_out_tm", [T, D], F32)
    io3["l0b_fused_order"] = True
    io3["l0b_kg_own"] = sc("l0b_kg_own", [T, 8], F32)
    build_l0_ffn_B(em, c, io3)
    ag_out = sc("ag_out", [NCORES * T, D], F32)
    em.collective("AllGather", ALU.bypass, [io3["l0b_out_tm"]], [ag_out])
    ffn0_tm = sc("ffn0_tm", [T, D], F32)
    tok_ag2 = ag_out.last_w

    def slice_out(E, sv, mk):
        for r in range(8):
            mk(E.dma_start(out=ffn0_tm.h[r * NS:(r + 1) * NS, :],
                           in_=ag_out.h[bass.ds(sv * T + r * T // 8 * 0 + sv * 0, NS), :]
                           if False else
                           ag_out.h.rearrange("(r t) d -> r t d", r=NCORES)[r][bass.ts(sv, NS), :]))
    em.raw_wait("scalar", tok_ag2)
    tok_sw2 = em.dyn_region("scalar", slice_out, 8)
    ffn0_tm.last_w = tok_sw2
    with Phase(em) as ph:
        tin = ph.sbuf("ftr_in", [128, D], F32)
        tev = ph.sbuf("ftr_ev", [128, 128], F32)
        pst = ph.psum([128, 128])
        for tt_ in range(T // 128):
            em.load(tin, ffn0_tm[tt_ * 128:(tt_ + 1) * 128, :])
            for j in range(NC):
                em.transpose(pst, tin[:, j * 128:(j + 1) * 128], c["ident"])
                em.copy(tev, pst)
                em.store(ffn_o[j * 128:(j + 1) * 128, tt_ * 128:(tt_ + 1) * 128], tev)
    xs = [io["x_in"], sc("x_l1", [D, T], F32), sc("x_l2", [D, T], F32), sc("x_l3", [D, T], F32)]
    for layer in (1, 2, 3):
        add3_fm(em, xs[layer], xs[layer - 1], attn_o, ffn_o)
        ln_fm(em, c, xs[layer], xn_fm, xn_bf, dst_lo=xn_lo)
        ioL = dict(io)
        ioL["xn_fm"] = xn_fm
        ioL["xn_bf"] = xn_bf
        ioL["xn_lo"] = xn_lo
        if layer == 1:
            _scratch_l1a(em, ioL)
            ioL["out_fm"] = attn_o
            build_l1_attn(em, c, ioL)
            ioL["out_fm"] = ffn_o
            build_l1_ffn(em, c, ioL)
        elif layer == 2:
            _scratch_l2(em, ioL)
            ioL["out_fm"] = attn_o
            build_l2_attn(em, c, ioL)
            ioL["out_fm"] = ffn_o
            build_l2_ffn(em, c, ioL)
        else:
            _scratch_l3(em, ioL)
            ioL["out_fm"] = attn_o
            build_l3_attn(em, c, ioL)
            ioL["out_fm"] = ffn_o
            build_l3_ffn(em, c, ioL)
    add3_fm(em, io["x_out"], xs[3], attn_o, ffn_o)
    em.finalize()
    return nc, io


def kernel(**inputs):
    W = _wprep(inputs)
    x0 = np.asarray(inputs["L0_attn_x"], np.float32)
    nc, io = build_fused({k: v.shape for k, v in W.items()})
    in_maps = []
    for cid in range(NCORES):
        xns = np.concatenate([x0[l, cid * NS:(cid + 1) * NS, :] for l in range(8)], 0)
        m = {"x_in": np.ascontiguousarray(x0[cid].T),
             "x_ns": np.ascontiguousarray(xns.T)}
        for k, v in W.items():
            m[k] = v
        in_maps.append(m)
    res = _run(nc, in_maps)
    out = np.stack([np.asarray(res.results[cid]["x_out"]).T for cid in range(NCORES)], 0)
    return np.ascontiguousarray(out).astype(np.float32)


if __name__ == "__main__":
    print("kernel.py framework module")
